# version 16
# baseline (speedup 1.0000x reference)
import math
import ml_dtypes
import numpy as np
from contextlib import ExitStack

import concourse.bass as bass
import concourse.mybir as mybir
from concourse.bass_utils import run_bass_kernel_spmd

F32 = mybir.dt.float32
BF16 = mybir.dt.bfloat16
I32 = mybir.dt.int32
OP = mybir.AluOpType
AF = mybir.ActivationFunctionType

B, T, D = 4, 4096, 1024
TC = 2048
DC = D // 128
TTW = 512
NTT = TC // TTW
H, DH = 16, 64
NPAIR = 8
E = 8
KT = TC // 128
LN_EPS = 1e-5
MASK_NEG = -60.0
VOFF = 2 * NPAIR * TC
_UPTO = None
_MOE_GATES_ONLY = False
QOFF = NPAIR * TC
KVW2 = 2 * NPAIR * TC + KT * H * 65
RVOFF = NPAIR * 1024
RKW = NPAIR * 1024 + 8 * H * 65


class P:

    def __init__(self, nc, block, est):
        self.nc, self.block, self.est = nc, block, est
        self.s = {}
        self.n = {}
        self.t = {}

    def sem(self, name):
        if name not in self.s:
            self.s[name] = self.est.enter_context(self.nc.semaphore(name))
            self.n[name] = 0
        return self.s[name]

    DMA_SEMS = ("dW", "dG", "ws", "lp", "wp", "hl", "hw", "wl", "qs", "vs",
                "ob", "gs", "gi", "hb", "cl")

    def inc(self, instr, name, k=None):
        if k is None:
            k = 16 if name.rstrip("0123") in self.DMA_SEMS else 1
        self.n[name] += k
        instr.then_inc(self.s[name], k)
        return self.n[name]

    def w(self, eng, name, val=None):
        eng.wait_ge(self.s[name], self.n[name] if val is None else val)

    def bar(self, eng, names):
        for nm in names:
            eng.wait_ge(self.s[nm], self.n[nm])


SEMS = (["dW", "dG", "tE", "vE", "sE", "cc", "wb", "xe", "ex", "mk", "ev", "wg", "wg2", "gB"]
        + [f"ws{i}" for i in range(2)] + [f"lp{i}" for i in range(2)] + [f"wp{i}" for i in range(2)]
        + [f"hl{i}" for i in range(2)] + [f"hw{i}" for i in range(2)] + [f"wl{i}" for i in range(2)]
        + [f"qs{i}" for i in range(4)] + [f"vs{i}" for i in range(2)] + [f"ob{i}" for i in range(2)]
        + [f"gs{i}" for i in range(2)] + [f"gi{i}" for i in range(2)] + [f"hb{i}" for i in range(2)]
        + [f"cl{i}" for i in range(2)])


def build_program():
    nc = bass.Bass("TRN2", target_bir_lowering=False)

    def din(name, shape, dt=F32):
        return nc.dram_tensor(name, shape, dt, kind="ExternalInput").ap()

    io = {}
    io["x_in"] = din("x_in", [D, TC])
    io["wqkv_s"] = din("wqkv_s", [128, 24576])
    io["wo_s"] = din("wo_s", [128, 8192])
    io["win_s"] = din("win_s", [128, 16384])
    io["wout_s"] = din("wout_s", [128, 8192])
    io["gw_s"] = din("gw_s", [128, 64])
    io["gb_s"] = din("gb_s", [1, 8])
    io["eb_s"] = din("eb_s", [8, 1024])
    io["moew_s"] = din("moew_s", [128, 65536])
    io["theta_s"] = din("theta_s", [128, 8192])
    io["thn_s"] = din("thn_s", [128, 8192])
    io["phi_s"] = din("phi_s", [128, 8192])
    io["phn_s"] = din("phn_s", [128, 8192])
    io["convd_s"] = din("convd_s", [128, 6144])
    io["ap_s"] = din("ap_s", [128, 8])
    io["lnw_s"] = din("lnw_s", [128, 8])
    io["lnb_s"] = din("lnb_s", [128, 8])
    io["pos_s"] = din("pos_s", [1, TC])
    io["frq_s"] = din("frq_s", [128, 1])
    io["sgn_s"] = din("sgn_s", [128, 1])
    io["rsc_s"] = din("rsc_s", [128, TC])
    io["rcc_s"] = din("rcc_s", [128, TC])
    io["rbias_s"] = din("rbias_s", [128, 1])
    io["lmask_s"] = din("lmask_s", [128, 1])
    io["rmask_s"] = din("rmask_s", [128, 1])
    io["cmask_s"] = din("cmask_s", [128, 1])
    io["nbr_s"] = din("nbr_s", [1, 1], I32)
    io["rtb_s"] = din("rtb_s", [1, 1], I32)
    io["dmask_s"] = din("dmask_s", [128, 2048])
    io["eye_s"] = din("eye_s", [128, 128])
    io["y_out"] = nc.dram_tensor("y_out", [D, TC], F32, kind="ExternalOutput").ap()

    io["wscr_moe"] = nc.dram_tensor("wscr_moe", [128, 65536], BF16, kind="Internal")
    io["wscr_hol"] = nc.dram_tensor("wscr_hol", [128, 16384], BF16, kind="Internal")
    io["kqstage"] = nc.dram_tensor("kqstage", [128, 2 * NPAIR * TC], BF16, kind="Internal")
    io["vstage0"] = nc.dram_tensor("vstage0", [128, KT * 8 * 65], BF16, kind="Internal")
    io["vstage1"] = nc.dram_tensor("vstage1", [128, KT * 8 * 65], BF16, kind="Internal")
    io["ckqo"] = nc.dram_tensor("ckqo", [2, 128, 2 * NPAIR * TC], BF16, kind="Internal")
    io["cvo0"] = nc.dram_tensor("cvo0", [2, 128, KT * 8 * 65], BF16, kind="Internal")
    io["cvo1"] = nc.dram_tensor("cvo1", [2, 128, KT * 8 * 65], BF16, kind="Internal")
    io["rectkv"] = nc.dram_tensor("rectkv", [128, RKW], BF16, kind="Internal")
    for _sg in range(4):
        io[f"rectst{_sg}"] = nc.dram_tensor(f"rectst{_sg}", [128, 8192], BF16,
                                            kind="Internal")
        io[f"cc3o{_sg}"] = nc.dram_tensor(f"cc3o{_sg}", [2, 128, 8192], BF16,
                                          kind="Internal")
    io["gstage"] = nc.dram_tensor("gstage", [128, DC * TC], BF16, kind="Internal")
    io["wmskd"] = nc.dram_tensor("wmskd", [8, TC], BF16, kind="Internal")
    io["h3a_in"] = nc.dram_tensor("h3a_in", [128, 32], F32, kind="Internal")
    io["h3a_out"] = nc.dram_tensor("h3a_out", [2, 128, 32], F32, kind="Internal")
    io["h3b_in"] = nc.dram_tensor("h3b_in", [128, 8], F32, kind="Internal")
    io["h3b_out"] = nc.dram_tensor("h3b_out", [2, 128, 8], F32, kind="Internal")

    est = ExitStack()
    p = P(nc, None, est)
    t = p.t
    sb = lambda name, shape, dt: t.__setitem__(name, est.enter_context(nc.sbuf_tensor(name, shape, dt)))
    psm = lambda name, shape: t.__setitem__(name, est.enter_context(nc.psum_tensor(name, shape, F32)))

    sb("X", [128, DC * TC], F32)
    sb("LNB", [128, DC * TC], BF16)
    sb("AR", [128, 35904], BF16)
    sb("EBB", [8, 1024], BF16)
    sb("WS", [128, 2 * 2048], F32)
    sb("WSB", [128, 2 * 2048], BF16)
    sb("ROWS", [1, 2 * TTW], F32)
    sb("XSQ", [128, TTW], BF16)
    sb("TF", [128, TTW], F32)
    sb("TF2", [128, TTW], F32)
    sb("SMF", [128, 256], F32)
    sb("SMB", [128, 2304], BF16)
    sb("SM1F", [1, 144], F32)
    sb("SM1B", [1, 144], BF16)
    sb("NBRI", [1, 1], I32)
    sb("RTBI", [1, 1], I32)
    psm("PSA", [128, 1024])
    psm("PSB", [128, 1024])
    psm("PSV", [128, 1024])
    psm("PSW", [128, 1024])

    t["FRQ"] = t["SMF"][:, 0:1]
    t["SGN"] = t["SMF"][:, 1:2]
    t["RBIAS"] = t["SMF"][:, 2:3]
    t["ZEROC"] = t["SMF"][:, 3:4]
    t["LMC"] = t["SMF"][:, 4:5]
    t["RMC"] = t["SMF"][:, 5:6]
    t["CMC"] = t["SMF"][:, 6:7]
    t["EPSC"] = t["SMF"][0:1, 7:8]
    t["HPI"] = t["SMF"][:, 8:9]
    t["ACOL"] = t["SMF"][:, 16:24]
    t["APC"] = t["SMF"][:, 24:32]
    t["LNWC"] = t["SMF"][:, 32:40]
    t["LNBC"] = t["SMF"][:, 40:48]
    t["CARR"] = t["SMF"][:, 48:56]
    t["R3B"] = t["SMF"][:, 56:64]
    t["EDGE"] = t["SMF"][:, 64:96]
    t["R3A"] = t["SMF"][:, 96:128]
    t["ONESF"] = t["SMF"][:, 128:256]
    t["EYEB"] = t["SMB"][:, 0:128]
    t["ONEB"] = t["SMB"][:, 128:640]
    t["WTM"] = t["SMB"][:, 640:656]
    t["GWB"] = t["SMB"][:, 656:720]
    t["CVDS"] = t["SMB"][:, 768:2304]
    t["ONEC"] = t["SM1F"][:, 0:128]
    t["GBF"] = t["SM1F"][:, 128:136]
    t["ONER"] = t["SM1B"][:, 0:128]
    t["GBB"] = t["SM1B"][:, 128:136]

    AR = t["AR"]
    ab = lambda off, n: AR[:, off:off + n]
    af = lambda off, n: AR[:, off:off + 2 * n].bitcast(F32)
    t["RSC"] = af(0, 2048)
    t["RSC2"] = af(4096, 2048)
    t["QRAW"] = af(8192, 1024)
    t["POSR"] = af(10240, 2048)[0:1, :]
    t["BNC"] = ab(14336, 2048)
    t["VBO"] = ab(16384, 2080)
    t["KG"] = ab(0, 4096)
    t["QG"] = ab(4096, 4096)
    t["KR"] = ab(8192, 4096)
    t["VG"] = ab(12288, 4160)
    t["VR"] = ab(16448, 4160)
    t["PT"] = ab(20608, 4096)
    t["ZR"] = af(24704, 1024)
    t["DMSK"] = t["WS"][:, 0:2048]
    t["OB"] = ab(26752, 1024)
    t["QR"] = ab(27776, 4096)
    t["RP"] = ab(31872, 2048)
    t["PL0"] = ab(8192 + 2048, 2048)
    t["PL1"] = ab(16448 + 2080, 2048)
    t["XE"] = ab(0, 8192)
    t["WBC"] = ab(8192, 4096)
    t["WMSK"] = ab(12288, 2048)[0:8, :]
    t["WG"] = ab(14336, 8192)
    t["WROW"] = af(22528, 2048)[0:8, :]
    t["VALB"] = ab(0, 16416)
    t["YS"] = ab(16448, 16384)
    t["GVB"] = ab(32832, 2048)
    t["GIN"] = ab(34880, 1024)
    t["POWT"] = ab(32832, 2048)

    for s in SEMS:
        p.sem(s)

    phases = [
        ("init", lambda c: ph_init(p, io, c)),
        ("ln1", lambda c: ph_ln(p, io, c, "ln1")),
        ("moe1g", lambda c: ph_moe_gates(p, io, c, "moe1")),
        ("prep", lambda c: ph_prep(p, io, c)),
        ("moe1", lambda c: ph_moe(p, io, c, "moe1")),
        ("ln2", lambda c: ph_ln(p, io, c, "ln2")),
        ("qkv", lambda c: ph_qkv(p, io, c)),
        ("attn", lambda c: ph_attn(p, io, c)),
        ("outproj", lambda c: ph_outproj(p, io, c)),
        ("ln3", lambda c: ph_ln(p, io, c, "ln3")),
        ("moe2", lambda c: ph_moe(p, io, c, "moe2")),
        ("ln4", lambda c: ph_ln(p, io, c, "ln4")),
        ("hol1", lambda c: ph_hol(p, io, c, "hol1")),
        ("ln5", lambda c: ph_ln(p, io, c, "ln5")),
        ("mixer", lambda c: ph_mixer(p, io, c)),
        ("ln6", lambda c: ph_ln(p, io, c, "ln6")),
        ("hol2", lambda c: ph_hol(p, io, c, "hol2")),
    ]
    with est:
        with nc.Block() as block:
            p.block = block
            ctx = {}
            for name, fn in phases:
                fn(ctx)
                if _UPTO is not None and name == _UPTO:
                    break
            ph_store(p, io, ctx)
    return nc


def stream_cast(p, io, ctx, src_name, npieces, tag, src_off=0, wait_dW=None,
                extra_waits=(), defer_casts=False):
    t = p.t
    wg0 = p.n["wg"]
    wsb = [p.n["ws0"], p.n["ws1"]]

    def b_sync(sync):
        p.bar(sync, ["sE", "tE", "vE"])
        if wait_dW is not None:
            p.w(sync, "dW", wait_dW)
        for i in range(npieces):
            par = i % 2
            if i >= 2:
                p.w(sync, "wg", wg0 + i - 1)
            p.inc(sync.dma_start(out=t["WS"][:, par * 2048:(par + 1) * 2048],
                                 in_=io[src_name][:, src_off + i * 2048: src_off + (i + 1) * 2048]),
                  f"ws{par}")

    p.block.sync(b_sync)
    s0 = p.n["sE"]
    wb0 = p.n["wb"]
    cast_wb = {i: wb0 + i + 1 for i in range(npieces)}

    def cast_piece(i):
        def b_one(scalar):
            par = i % 2
            if i < 2:
                for nm, val in extra_waits:
                    p.w(scalar, nm, val)
            p.w(scalar, f"ws{par}", wsb[par] + 16 * (i // 2 + 1))
            if i >= 2:
                p.w(scalar, "wg", wg0 + i - 1)
            scalar.copy(t["WSB"][:, par * 2048:(par + 1) * 2048],
                        t["WS"][:, par * 2048:(par + 1) * 2048])
            p.inc(scalar.drain(), "wb")

        p.block.scalar(b_one)

    if not defer_casts:
        for i in range(npieces):
            cast_piece(i)
    ctx[f"{tag}_cast0"] = s0
    ctx[f"{tag}_wg0"] = wg0
    return s0, wg0, cast_piece, cast_wb


def ph_init(p, io, ctx):
    t, block = p.t, p.block

    def b_sync(sync):
        for c in range(DC):
            p.inc(sync.dma_start(out=t["X"][:, c * TC:(c + 1) * TC],
                                 in_=io["x_in"][c * 128:(c + 1) * 128, :]), "dW")
        for dst, src in [("FRQ", "frq_s"), ("SGN", "sgn_s"), ("RBIAS", "rbias_s"),
                         ("LMC", "lmask_s"), ("RMC", "rmask_s"), ("CMC", "cmask_s"),
                         ("APC", "ap_s"), ("LNWC", "lnw_s"), ("LNBC", "lnb_s"),
                         ("NBRI", "nbr_s"), ("RTBI", "rtb_s"), ("GBF", "gb_s")]:
            p.inc(sync.dma_start(out=t[dst][:], in_=io[src]), "dW")
        p.inc(sync.dma_start(out=t["WS"][:, 0:64], in_=io["gw_s"]), "dW")
        p.inc(sync.dma_start(out=t["WS"][0:8, 64:1088], in_=io["eb_s"]), "dW")
        p.inc(sync.dma_start(out=t["TF"][:, 0:128], in_=io["eye_s"]), "dW")

    block.sync(b_sync)
    ld0 = p.n["dW"]

    def b_scalar(scalar):
        p.w(scalar, "dW", ld0)
        scalar.copy(t["GWB"][:], t["WS"][:, 0:64])
        scalar.copy(t["EBB"][:], t["WS"][0:8, 64:1088])
        scalar.copy(t["EYEB"][:], t["TF"][:, 0:128])
        scalar.copy(t["GBB"][:], t["GBF"][:])
        p.inc(scalar.drain(), "sE")

    block.scalar(b_scalar)
    ctx["init_s"] = p.n["sE"]

    def b_vector(vector):
        for name, val in [("ONEC", 1.0), ("ONER", 1.0), ("ONEB", 1.0), ("ZEROC", 0.0),
                          ("EPSC", LN_EPS), ("HPI", float(np.pi / 2))]:
            vector.memset(t[name][:], val)
        vector.memset(t["ONESF"][:], 1.0)
        p.inc(vector.drain(), "vE")

    block.vector(b_vector)
    ctx["init_v"] = p.n["vE"]


def ph_prep(p, io, ctx):
    t, block = p.t, p.block

    s0 = p.n["sE"]
    lpb = [p.n["lp0"], p.n["lp1"]]
    wpb = [p.n["wp0"], p.n["wp1"]]

    def _prep_q(eng, par):
        if par == 0:
            p.w(eng, "sE", ctx["init_s"])
        for i in range(par, 32, 2):
            if i >= 2:
                p.w(eng, "sE", s0 + i - 1)
            p.inc(eng.dma_start(out=t["WS"][:, par * 2048:(par + 1) * 2048],
                                in_=io["moew_s"][:, i * 2048:(i + 1) * 2048]), f"lp{par}")
            if i >= 2:
                j = i - 2
                p.w(eng, "sE", s0 + j + 1)
                p.inc(eng.dma_start(out=io["wscr_moe"][:, j * 2048:(j + 1) * 2048],
                                    in_=t["WSB"][:, par * 2048:(par + 1) * 2048]),
                      f"wp{par}")
        j = 30 + par
        p.w(eng, "sE", s0 + j + 1)
        p.inc(eng.dma_start(out=io["wscr_moe"][:, j * 2048:(j + 1) * 2048],
                            in_=t["WSB"][:, par * 2048:(par + 1) * 2048]),
              f"wp{par}")

    def b_sync2(sync):
        _prep_q(sync, 0)

    def b_gp2(gp):
        p.w(gp, "sE", ctx["init_s"])
        _prep_q(gp, 1)

    def b_scalar2(scalar):
        for i in range(32):
            par = i % 2
            p.w(scalar, f"lp{par}", lpb[par] + 16 * (i // 2 + 1))
            if i >= 2:
                j = i - 2
                p.w(scalar, f"wp{j % 2}", wpb[j % 2] + 16 * (j // 2 + 1))
            scalar.copy(t["WSB"][:, par * 2048:(par + 1) * 2048],
                        t["WS"][:, par * 2048:(par + 1) * 2048])
            p.inc(scalar.drain(), "sE")

    block.sync(b_sync2)
    block.gpsimd(b_gp2)
    block.scalar(b_scalar2)
    ctx["moe_scr_wb"] = {j: wpb[j % 2] + 16 * (j // 2 + 1) for j in range(32)}
    ctx["moe_scr_sem"] = {j: f"wp{j % 2}" for j in range(32)}

    hv0 = p.n["vE"]
    hs0 = p.n["sE"]
    hl0 = p.n["hl0"]
    hwb = [p.n["hw0"], p.n["hw1"]]

    def b_sync4(sync):
        p.w(sync, "sE", hs0)
        for k in range(8):
            w, i = divmod(k, 4)
            a_s = io["theta_s"] if w == 0 else io["phi_s"]
            n_s = io["thn_s"] if w == 0 else io["phn_s"]
            if k >= 1:
                p.w(sync, "sE", hs0 + k)
            p.inc(sync.dma_start(out=t["WS"][:, 0:2048], in_=a_s[:, i * 2048:(i + 1) * 2048]), "hl0")
            p.inc(sync.dma_start(out=t["WS"][:, 2048:4096], in_=n_s[:, i * 2048:(i + 1) * 2048]), "hl0")
            if k >= 2:
                j = k - 2
                p.w(sync, "sE", hs0 + j + 1)
                p.inc(sync.dma_start(out=io["wscr_hol"][:, j * 2048:(j + 1) * 2048],
                                     in_=t["WSB"][:, (j % 2) * 2048:((j % 2) + 1) * 2048]),
                      f"hw{j % 2}")
        for j in (6, 7):
            p.w(sync, "sE", hs0 + j + 1)
            p.inc(sync.dma_start(out=io["wscr_hol"][:, j * 2048:(j + 1) * 2048],
                                 in_=t["WSB"][:, (j % 2) * 2048:((j % 2) + 1) * 2048]),
                  f"hw{j % 2}")

    def b_vector2(vector):
        for k in range(8):
            p.w(vector, "hl0", hl0 + 32 * (k + 1))
            if k >= 1:
                p.w(vector, "sE", hs0 + k)
            vector.tensor_add(t["WS"][:, 0:2048], t["WS"][:, 0:2048], t["WS"][:, 2048:4096])
            p.inc(vector.drain(), "vE")

    def b_scalar3(scalar):
        for k in range(8):
            w, i = divmod(k, 4)
            p.w(scalar, "vE", hv0 + k + 1)
            if k < 2:
                p.w(scalar, ctx["moe_scr_sem"][30 + k], ctx["moe_scr_wb"][30 + k])
            if k >= 2:
                j = k - 2
                p.w(scalar, f"hw{j % 2}", hwb[j % 2] + 16 * (j // 2 + 1))
            par = k % 2
            if w == 0:
                scalar.activation(t["WSB"][:, par * 2048:(par + 1) * 2048],
                                  t["WS"][:, 0:2048], AF.Sin, bias=t["HPI"][:], scale=1.0)
            else:
                scalar.activation(t["WSB"][:, par * 2048:(par + 1) * 2048],
                                  t["WS"][:, 0:2048], AF.Sin)
            p.inc(scalar.drain(), "sE")

    block.sync(b_sync4)
    block.vector(b_vector2)
    block.scalar(b_scalar3)
    ctx["hol_scr_hw"] = {j: (f"hw{j % 2}", hwb[j % 2] + 16 * (j // 2 + 1)) for j in range(8)}
    ctx["prep_done_dW"] = p.n["dW"]
    ctx["prep_done_sE"] = p.n["sE"]


def ph_ln(p, io, ctx, tag, affine=False, rsl=None, extra_residual=False):
    t, block = p.t, p.block
    X, LNB = t["X"], t["LNB"]
    stats = t["PSW"][:, 0:512]

    if rsl is None:
        def b_cast(scalar):
            p.bar(scalar, ["vE", "tE"])
            for c in range(DC):
                scalar.copy(LNB[:, c * TC:(c + 1) * TC], X[:, c * TC:(c + 1) * TC])
            p.inc(scalar.drain(), "sE")

        block.scalar(b_cast)

    for tt in range(NTT):
        tok = slice(tt * TTW, (tt + 1) * TTW)
        src = (lambda c: LNB[:, c * TC + tt * TTW: c * TC + (tt + 1) * TTW]) if rsl is None \
            else (lambda c: rsl(c)[:, tok])
        xf = (lambda c: X[:, c * TC + tt * TTW: c * TC + (tt + 1) * TTW]) if rsl is None \
            else (lambda c: rsl(c)[:, tok])

        fast = rsl is None or tag == "ln7"
        sqd = t["KG"] if tag == "ln1" else t["WSB"]
        if fast:
            def b_sq(scalar, src=src, tt=tt, sqd=sqd):
                if tt == 0:
                    p.bar(scalar, ["sE", "tE", "vE"])
                    gates = {}
                    for nm, val in ctx.get("hol_scr_hw", {}).values():
                        gates[nm] = max(gates.get(nm, 0), val)
                    for j, val in ctx.get("moe_scr_wb", {}).items():
                        nm = ctx["moe_scr_sem"][j]
                        gates[nm] = max(gates.get(nm, 0), val)
                    for nm, val in gates.items():
                        p.w(scalar, nm, val)
                else:
                    p.w(scalar, "tE", ctx[f"{tag}_stats_t"][tt - 1])
                for c in range(DC):
                    scalar.activation(sqd[:, c * TTW:(c + 1) * TTW], src(c),
                                      AF.Square)
                p.inc(scalar.drain(), "sE")

            block.scalar(b_sq)
            sqv = p.n["sE"]

            def b_stats(tensor, src=src, sqv=sqv, sqd=sqd):
                p.w(tensor, "sE", sqv)
                for c in range(DC):
                    tensor.matmul(stats[0:1, :], t["ONEB"][:, 0:1], src(c),
                                  start=(c == 0), stop=(c == DC - 1))
                    tensor.matmul(t["PSW"][0:1, 512:1024], t["ONEB"][:, 0:1],
                                  sqd[:, c * TTW:(c + 1) * TTW],
                                  start=(c == 0), stop=(c == DC - 1))
                p.inc(tensor.drain(), "tE")

            block.tensor(b_stats)
            ctx.setdefault(f"{tag}_stats_t", {})[tt] = p.n["tE"]
        else:
            sq0 = p.n["sE"]

            def b_sq(scalar, src=src):
                p.bar(scalar, ["sE", "tE", "vE"])
                for c in range(DC):
                    if c >= 1:
                        p.w(scalar, "tE", ctx[f"{tag}_st0"] + c)
                    scalar.activation(t["XSQ"][:, :], src(c), AF.Square)
                    p.inc(scalar.drain(), "sE")

            ctx[f"{tag}_st0"] = p.n["tE"]
            block.scalar(b_sq)

            def b_stats(tensor, src=src, sq0=sq0):
                for c in range(DC):
                    p.w(tensor, "sE", sq0 + c + 1)
                    tensor.matmul(stats[0:1, :], t["ONEB"][:, 0:1], src(c),
                                  start=(c == 0), stop=(c == DC - 1))
                    tensor.matmul(t["PSW"][0:1, 512:1024], t["ONEB"][:, 0:1], t["XSQ"][:, :],
                                  start=(c == 0), stop=(c == DC - 1))
                    p.inc(tensor.drain(), "tE")

            block.tensor(b_stats)

        X0 = t["ROWS"][0:1, 0:512]
        X1 = t["ROWS"][0:1, 512:1024]

        msq = (t["PSV"] if fast else t["PSA"])[0:1, 0:512]

        def b_r1(scalar, msq=msq):
            p.bar(scalar, ["tE"])
            scalar.activation(X0, stats[0:1, :], AF.Identity, scale=1.0 / D)
            scalar.activation(X1, t["PSW"][0:1, 512:1024], AF.Identity, scale=1.0 / D)
            scalar.drain()
            scalar.activation(msq, X0, AF.Square)
            p.inc(scalar.drain(), "sE")

        block.scalar(b_r1)

        def b_r3(vector, msq=msq):
            p.bar(vector, ["sE"])
            vector.scalar_tensor_tensor(X1, msq, -1.0, X1,
                                        OP.mult, OP.add)
            p.inc(vector.drain(), "vE")

        block.vector(b_r3)

        def b_r4(scalar):
            p.bar(scalar, ["vE"])
            scalar.activation(X1, X1, AF.Sqrt, bias=t["EPSC"][:], scale=1.0)
            p.inc(scalar.drain(), "sE")

        block.scalar(b_r4)

        def b_r5(vector):
            p.bar(vector, ["sE"])
            vector.reciprocal(X1, X1)
            vector.drain()
            vector.scalar_tensor_tensor(X0, X0, -1.0, X1, OP.mult, OP.mult)
            p.inc(vector.drain(), "vE")

        block.vector(b_r5)

        def b_bc(tensor):
            p.bar(tensor, ["vE"])
            tensor.matmul(t["PSA"][:, 0:512], t["ONESF"][0:1, 0:128], X1,
                          start=True, stop=True)
            tensor.matmul(t["PSB"][:, 0:512], t["ONESF"][0:1, 0:128], X0,
                          start=True, stop=True)
            p.inc(tensor.drain(), "tE")

        block.tensor(b_bc)

        def b_nm(vector, src=src, xf=xf):
            p.bar(vector, ["tE"])
            if rsl is None:
                for c in range(0, DC, 2):
                    tfa = t["TF"][:, :]
                    tfb = t["TF2"][:, 0:TTW]
                    csa = slice(c * TC + tt * TTW, c * TC + (tt + 1) * TTW)
                    csb = slice((c + 1) * TC + tt * TTW, (c + 1) * TC + (tt + 1) * TTW)
                    vector.tensor_mul(tfa, xf(c), t["PSA"][:, 0:512])
                    vector.tensor_mul(tfb, xf(c + 1), t["PSA"][:, 0:512])
                    vector.drain()
                    vector.tensor_add(LNB[:, csa], tfa, t["PSB"][:, 0:512])
                    vector.tensor_add(LNB[:, csb], tfb, t["PSB"][:, 0:512])
                    vector.drain()
                p.inc(vector.drain(), "vE")
                return
            for c in range(DC):
                tf = t["TF"][:, :]
                if False:
                    pass
                else:
                    cs = slice(c * TC + tt * TTW, c * TC + (tt + 1) * TTW)
                    tf2 = t["TF2"][:, 0:TTW]
                    vector.tensor_mul(tf, rsl(c)[:, tok], t["PSA"][:, 0:512])
                    vector.drain()
                    vector.tensor_add(tf2, tf, t["PSB"][:, 0:512])
                    vector.drain()
                    vector.tensor_scalar(tf, tf2, t["LNWC"][:, c:c + 1],
                                         t["LNBC"][:, c:c + 1], OP.mult, OP.add)
                    vector.drain()
                    vector.tensor_add(t["X"][:, cs], t["X"][:, cs], tf)
                    vector.drain()
            p.inc(vector.drain(), "vE")

        block.vector(b_nm)

    ctx[f"{tag}_done_v"] = p.n["vE"]


def ph_moe_gates(p, io, ctx, tag):
    t, block = p.t, p.block
    X, LNB = t["X"], t["LNB"]
    PSA, PSB, PSV, PSW = t["PSA"], t["PSB"], t["PSV"], t["PSW"]

    t1_v, v1_v, s1_v, v2_v, t2_v, s2_v = {}, {}, {}, {}, {}, {}
    for tt in range(NTT):
        for j in range(4):
            jt = tt * 4 + j
            par = jt % 2
            lg = PSW[:, 512 + par * 8:520 + par * 8]
            B = 16 + par * 64

            def b_t1(tensor, jt=jt, lg=lg):
                if jt == 0:
                    p.bar(tensor, ["vE"])
                if jt >= 2:
                    p.w(tensor, "vE", v2_v[jt - 2])
                for c in range(DC):
                    tensor.matmul(lg, LNB[:, c * TC + jt * 128: c * TC + (jt + 1) * 128],
                                  t["GWB"][:, c * 8:(c + 1) * 8], start=(c == 0), stop=False)
                tensor.matmul(lg, t["ONER"][:], t["GBB"][:], start=False, stop=True)
                p.inc(tensor.drain(), "tE")

            block.tensor(b_t1)
            t1_v[jt] = p.n["tE"]

            def b_v1(vector, jt=jt, lg=lg, B=B):
                p.w(vector, "tE", t1_v[jt])
                vector.reduce_max(t["TF"][:, B + 0:B + 1], lg, axis=mybir.AxisListType.X)
                vector.drain()
                vector.tensor_scalar(t["TF"][:, B + 8:B + 16], lg, t["TF"][:, B + 0:B + 1],
                                     None, OP.is_equal)
                vector.drain()
                vector.scalar_tensor_tensor(t["TF"][:, B + 16:B + 24], t["TF"][:, B + 8:B + 16],
                                            -1e9, lg, OP.mult, OP.add)
                vector.drain()
                vector.reduce_max(t["TF"][:, B + 1:B + 2], t["TF"][:, B + 16:B + 24],
                                  axis=mybir.AxisListType.X)
                vector.tensor_scalar_mul(t["TF"][:, B + 3:B + 4], t["TF"][:, B + 0:B + 1], -1.0)
                p.inc(vector.drain(), "vE")

            block.vector(b_v1)
            v1_v[jt] = p.n["vE"]

            def b_s1(scalar, jt=jt, lg=lg, B=B):
                p.w(scalar, "vE", v1_v[jt])
                scalar.activation(t["TF"][:, B + 32:B + 40], lg, AF.Exp,
                                  bias=t["TF"][:, B + 3:B + 4], scale=1.0,
                                  accum_out=t["TF"][:, B + 4:B + 5])
                p.inc(scalar.drain(), "sE")

            block.scalar(b_s1)
            s1_v[jt] = p.n["sE"]

            def b_v2(vector, jt=jt, lg=lg, B=B, par=par):
                p.w(vector, "sE", s1_v[jt])
                if jt >= 2:
                    p.w(vector, "tE", t2_v[jt - 2])
                w = t["WTM"][:, par * 8:(par + 1) * 8]
                vector.tensor_scalar(t["TF"][:, B + 24:B + 32], lg, t["TF"][:, B + 1:B + 2],
                                     None, OP.is_ge)
                vector.reciprocal(t["TF"][:, B + 5:B + 6], t["TF"][:, B + 4:B + 5])
                vector.drain()
                vector.tensor_scalar_mul(t["TF"][:, B + 6:B + 7], t["TF"][:, B + 5:B + 6], 0.5)
                vector.drain()
                vector.tensor_mul(t["TF"][:, B + 24:B + 32], t["TF"][:, B + 32:B + 40],
                                  t["TF"][:, B + 24:B + 32])
                vector.drain()
                vector.tensor_scalar(w, t["TF"][:, B + 24:B + 32], t["TF"][:, B + 6:B + 7],
                                     None, OP.mult)
                p.inc(vector.drain(), "vE")

            block.vector(b_v2)
            v2_v[jt] = p.n["vE"]

            def b_t2(tensor, tt=tt, j=j, jt=jt, par=par):
                p.w(tensor, "vE", v2_v[jt])
                if tt >= 1:
                    p.w(tensor, "sE", s2_v[tt - 1])
                tensor.matmul(PSV[0:8, j * 128:(j + 1) * 128],
                              t["WTM"][:, par * 8:(par + 1) * 8],
                              t["EYEB"][:], start=True, stop=True)
                p.inc(tensor.drain(), "tE")

            block.tensor(b_t2)
            t2_v[jt] = p.n["tE"]

        def b_s2(scalar, tt=tt):
            p.w(scalar, "tE", t2_v[tt * 4 + 3])
            scalar.copy(t["WROW"][:, tt * TTW:(tt + 1) * TTW], PSV[0:8, 0:512])
            p.inc(scalar.drain(), "sE")

        block.scalar(b_s2)
        s2_v[tt] = p.n["sE"]

        def b_v3(vector, tt=tt):
            p.w(vector, "sE", s2_v[tt])
            vector.tensor_copy(t["WMSK"][:, tt * TTW:(tt + 1) * TTW],
                               t["WROW"][:, tt * TTW:(tt + 1) * TTW])
            p.inc(vector.drain(), "vE")

        block.vector(b_v3)

    ctx[f"{tag}_gates_v"] = p.n["vE"]

    if _MOE_GATES_ONLY:
        def b_fence0(vector):
            p.bar(vector, ["vE", "sE", "tE"])
            p.inc(vector.drain(), "vE")
        block.vector(b_fence0)
        ctx[f"{tag}_done_v"] = p.n["vE"]
        return

    wm_d0 = p.n["dW"]

    def b_wmw(sync):
        p.bar(sync, ["vE"])
        p.inc(sync.dma_start(out=io["wmskd"][:, :], in_=t["WMSK"][0:8, :]), "dW")

    block.sync(b_wmw)
    ctx[f"{tag}_wm_dW"] = wm_d0 + 16


def ph_moe(p, io, ctx, tag):
    t, block = p.t, p.block
    X, LNB = t["X"], t["LNB"]
    PSA, PSB, PSV, PSW = t["PSA"], t["PSB"], t["PSV"], t["PSW"]

    if f"{tag}_gates_v" not in ctx:
        ph_moe_gates(p, io, ctx, tag)
        if _MOE_GATES_ONLY:
            return
    wm_d0 = ctx[f"{tag}_wm_dW"] - 16
    gB0 = p.n["gB"]

    def b_wmd(gp):
        p.w(gp, "dW", wm_d0 + 16)
        pend = [None]
        for tt in range(NTT):
            for e in range(E):
                idx = tt * E + e
                if tt > 0:
                    p.w(gp, "xe", ctx[f"{tag}_xe0"] + (tt - 1) * E + e + 1)
                srcap = bass.AP(io["wmskd"], e * TC + tt * TTW, [[0, 128], [1, TTW]])
                p.inc(gp.dma_start(out=t["WBC"][:, e * TTW:(e + 1) * TTW], in_=srcap),
                      f"qs{idx % 2}")
                cur = (f"qs{idx % 2}", p.n[f"qs{idx % 2}"])
                if pend[0] is not None:
                    p.w(gp, pend[0][0], pend[0][1])
                    gp.sem_inc(p.s["gB"], 1)
                    p.n["gB"] += 1
                pend[0] = cur
        p.w(gp, pend[0][0], pend[0][1])
        gp.sem_inc(p.s["gB"], 1)
        p.n["gB"] += 1

    ctx[f"{tag}_xe0"] = p.n["xe"]
    block.gpsimd(b_wmd)

    xe0 = ctx[f"{tag}_xe0"]
    wg0 = p.n["wg"]
    d0 = p.n["dW"]


    wlb = [p.n["wl0"], p.n["wl1"]]

    def b_wdma2(sync):
        for tt in range(NTT):
            for e in range(E):
                for half in range(2):
                    gidx = tt * 16 + e * 2 + half
                    par = gidx % 2
                    if gidx >= 2:
                        p.w(sync, "wg2", ctx[f"{tag}_wg2_0"] + gidx - 1)
                    gsrc = half * 8 + e
                    for j in (2 * gsrc, 2 * gsrc + 1):
                        p.w(sync, ctx["moe_scr_sem"][j], ctx["moe_scr_wb"][j])
                    p.inc(sync.dma_start(out=t["WG"][:, par * 4096:(par + 1) * 4096],
                                         in_=io["wscr_moe"][:, gsrc * 4096:(gsrc + 1) * 4096]),
                          f"wl{par}")

    ctx[f"{tag}_wg2_0"] = p.n["wg2"]
    block.sync(b_wdma2)

    def bank(half, oc):
        ps = (PSA, PSB, PSV, PSW)[half * 2 + oc // 2]
        return ps[:, (oc % 2) * 512:(oc % 2 + 1) * 512]

    def b_mm(tensor):
        for tt in range(NTT):
            tok = slice(tt * TTW, (tt + 1) * TTW)
            for e in range(E):
                epar = (tt * E + e) % 2
                p.w(tensor, "xe", xe0 + tt * E + e + 1)
                for half in range(2):
                    gidx = tt * 16 + e * 2 + half
                    par = gidx % 2
                    p.w(tensor, f"wl{par}", wlb[par] + 16 * (gidx // 2 + 1))
                    if e == 0 and half == 0 and tt > 0:
                        p.w(tensor, "ev", ctx[f"{tag}_ev0"] + 2 * tt)
                    for oc in range(4):
                        for kc in range(DC):
                            tensor.matmul(
                                bank(half, oc),
                                t["WG"][:, par * 4096 + (oc * DC + kc) * 128:
                                        par * 4096 + (oc * DC + kc) * 128 + 128],
                                t["XE"][:, (epar * DC + kc) * TTW:(epar * DC + kc + 1) * TTW],
                                start=(e == 0 and kc == 0), stop=False)
                    p.inc(tensor.drain(), "wg2")
                p.inc(tensor.drain(), "wg")
            for half in range(2):
                for oc in range(4):
                    o = half * 4 + oc
                    tensor.matmul(bank(half, oc), t["EBB"][0:8, o * 128:(o + 1) * 128],
                                  t["WMSK"][0:8, tok], start=False, stop=True)
                p.inc(tensor.drain(), "tE")
                ctx.setdefault(f"{tag}_half_t", []).append(p.n["tE"])

    ctx[f"{tag}_ev0"] = p.n["ev"]
    block.tensor(b_mm)
    half_ts = ctx[f"{tag}_half_t"]

    def ev_pass(vector, tt):
        for half in range(2):
            p.w(vector, "tE", half_ts[tt * 2 + half])
            for oc in range(4):
                o = half * 4 + oc
                cs = slice(o * TC + tt * TTW, o * TC + (tt + 1) * TTW)
                vector.tensor_add(X[:, cs], X[:, cs], bank(half, oc))
            p.inc(vector.drain(), "ev")

    def xe_one(vector, tt, e):
        idx = tt * E + e
        par = idx % 2
        p.w(vector, "gB", gB0 + idx + 1)
        if idx >= 2:
            p.w(vector, "wg", wg0 + idx - 1)
        for c in range(DC):
            vector.tensor_mul(
                t["XE"][:, (par * DC + c) * TTW:(par * DC + c + 1) * TTW],
                LNB[:, c * TC + tt * TTW: c * TC + (tt + 1) * TTW],
                t["WBC"][:, e * TTW:(e + 1) * TTW])
        p.inc(vector.drain(), "xe")

    def b_vec(vector):
        for tt in range(NTT):
            for e in range(E):
                xe_one(vector, tt, e)
                if tt >= 1 and e == 1:
                    ev_pass(vector, tt - 1)
        ev_pass(vector, NTT - 1)

    block.vector(b_vec)

    ctx[f"{tag}_done_ev"] = p.n["ev"]

    def b_fence(vector):
        p.w(vector, "ev", ctx[f"{tag}_done_ev"])
        p.inc(vector.drain(), "vE")

    block.vector(b_fence)
    ctx[f"{tag}_done_v"] = p.n["vE"]


def ph_qkv(p, io, ctx):
    t, block = p.t, p.block
    LNB = t["LNB"]
    PSA, PSB, PSW = t["PSA"], t["PSB"], t["PSW"]

    ln_done = ctx["ln2_done_v"]

    def b_aux(sync):
        p.bar(sync, ["tE", "vE", "ev", "sE"])
        p.inc(sync.dma_start(out=t["RSC"][:], in_=io["rsc_s"]), "dW")
        p.inc(sync.dma_start(out=t["RSC2"][:], in_=io["rcc_s"]), "dW")


    p.block.sync(b_aux)
    aux_dW = p.n["dW"]

    def b_auxc(scalar):
        p.w(scalar, "dW", aux_dW)
        scalar.copy(t["TF"][0:1, 0:1], t["RSC"][0:1, 0:1])
        p.inc(scalar.drain(), "sE")

    p.block.scalar(b_auxc)
    aux_s = p.n["sE"]

    s0, wg0, cast_piece, cast_sE = stream_cast(
        p, io, ctx, "wqkv_s", 12, "qkv", wait_dW=ctx["prep_done_dW"],
        extra_waits=tuple(ctx["hol_scr_hw"].values()), defer_casts=True)
    cast_piece(0)
    cast_piece(1)

    tab_s = p.n["sE"]
    ev0 = p.n["ev"]
    ctx["qkv_qs0"] = [p.n[f"qs{i}"] for i in range(4)]
    VS_SEMS = ("vs0", "vs1", "gs0", "gs1")
    ctx["qkv_vs0"] = [p.n[nm] for nm in VS_SEMS]
    ctx["qkv_vs_sems"] = VS_SEMS

    u = 0
    for wi, which in enumerate(("q", "k")):
        for pair in range(NPAIR):
            piece = wi * 4 + pair // 2
            par = piece % 2
            wcol0 = par * 2048 + (pair % 2) * 1024
            if pair % 2 == 0 and 2 <= piece + 1 < 12:
                cast_piece(piece + 1)
            for qb in range(4):
                bank = PSA[:, 0:512] if u % 2 == 0 else PSB[:, 0:512]

                def b_mm(tensor, piece=piece, wcol0=wcol0, bank=bank, qb=qb, u=u,
                         pair=pair):
                    p.w(tensor, "wb", cast_sE[piece])
                    p.w(tensor, "sE", tab_s)
                    p.w(tensor, "vE", ln_done)
                    if u >= 2:
                        p.w(tensor, "ev", ev0 + 2 * u - 3)
                    for kc in range(DC):
                        tensor.matmul(bank,
                                      t["WSB"][:, wcol0 + kc * 128: wcol0 + kc * 128 + 128],
                                      LNB[:, kc * TC + qb * TTW: kc * TC + (qb + 1) * TTW],
                                      start=(kc == 0), stop=(kc == DC - 1))
                    p.inc(tensor.drain(), "tE")
                    if qb == 3 and (pair % 2) == 1:
                        p.inc(tensor.drain(), "wg")

                block.tensor(b_mm)
                mm_t = p.n["tE"]

                def b_rope(vector, u=u, qb=qb, which=which, pair=pair,
                           bank=bank, mm_t=mm_t):
                    p.w(vector, "tE", mm_t)
                    if u >= 4:
                        uu4 = u - 4
                        p.w(vector, f"qs{uu4 % 4}", ctx["qkv_qs0"][uu4 % 4] + 16 * (uu4 // 4 + 1))
                    raw = t["QRAW"][:, (u % 2) * TTW:((u % 2) + 1) * TTW]
                    vector.tensor_copy(raw, bank)
                    p.inc(vector.drain(), "ev")
                    t1 = t["TF"][:, 0:TTW]
                    t2 = t["TF2"][:, 0:TTW]
                    vector.tensor_mul(t1, raw, t["RSC"][:, qb * TTW:(qb + 1) * TTW])
                    vector.drain()
                    perm = [(i + 1 if i % 2 == 0 else i - 1) for i in range(32)]
                    vector.stream_shuffle(t2, t1, perm)
                    vector.drain()
                    vector.tensor_mul(t1, raw, t["RSC2"][:, qb * TTW:(qb + 1) * TTW])
                    vector.drain()
                    slot = (u % 4) * TTW
                    vector.scalar_tensor_tensor(t["BNC"][:, slot:slot + TTW], t2,
                                                t["SGN"][:], t1, OP.mult, OP.add)
                    p.inc(vector.drain(), "ev")

                block.vector(b_rope)
                rp_v = p.n["ev"]

                def b_st(gp, which=which, pair=pair, qb=qb, rp_v=rp_v, u=u):
                    p.w(gp, "ev", rp_v)
                    slot = (u % 4) * TTW
                    if which == "q":
                        dst = io["kqstage"][:, QOFF + pair * TC + qb * TTW:
                                            QOFF + pair * TC + (qb + 1) * TTW]
                        src = t["BNC"][:, slot:slot + TTW]
                    else:
                        dst = io["kqstage"][:, 0:NPAIR * TC].rearrange(
                            "p (t w) -> p t w", w=1024
                        )[:, qb * 4:(qb + 1) * 4, pair * 128:(pair + 1) * 128]
                        src = t["BNC"][:, slot:slot + TTW].rearrange(
                            "p (t w) -> p t w", w=128)
                    p.inc(gp.dma_start(out=dst, in_=src),
                          f"qs{u % 4}")

                block.gpsimd(b_st)
                u += 1

    def b_cckq(gp):
        for i in range(4):
            p.w(gp, f"qs{i}", p.n[f"qs{i}"])
        p.inc(gp.collective_compute(
            "AllGather", OP.bypass,
            replica_groups=[[0, 1], [2, 3], [4, 5], [6, 7]],
            ins=[io["kqstage"].ap().opt()],
            outs=[io["ckqo"].ap().opt()],
        ), "cc")

    block.gpsimd(b_cckq)
    ctx["cc_kq"] = p.n["cc"]

    for grp in range(2):
        if grp == 0:
            cast_piece(9)
        else:
            cast_piece(10)
            cast_piece(11)
        for jt in range(KT):
            uu = grp * KT + jt
            bank = PSA[:, 0:512] if uu % 2 == 0 else PSB[:, 0:512]

            def b_vm(tensor, grp=grp, jt=jt, bank=bank, uu=uu):
                p.w(tensor, "wb", cast_sE[8 + grp * 2 + 1])
                p.w(tensor, "ev", ev0 + 128 + max(0, uu - 1))
                for kc in range(DC):
                    pc = 8 + grp * 2 + (kc // 4)
                    wcol = (pc % 2) * 2048 + (kc % 4) * 512
                    tensor.matmul(bank,
                                  LNB[:, kc * TC + jt * 128: kc * TC + (jt + 1) * 128],
                                  t["WSB"][:, wcol: wcol + 512],
                                  start=(kc == 0), stop=(kc == DC - 1))
                p.inc(tensor.drain(), "tE")
                if jt == KT - 1:
                    p.inc(tensor.drain(), "wg")
                    p.inc(tensor.drain(), "wg")

            block.tensor(b_vm)
            vm_t = p.n["tE"]

            def b_vev(vector, uu=uu, bank=bank, vm_t=vm_t):
                p.w(vector, "tE", vm_t)
                if uu >= 4:
                    sl = uu % 4
                    p.w(vector, ctx["qkv_vs_sems"][sl],
                        ctx["qkv_vs0"][sl] + 16 * (uu // 4))
                dstp = uu % 4
                dst = t["VBO"][:, dstp * 520: dstp * 520 + 520].rearrange("p (h w) -> p h w", w=65)
                src = bank.rearrange("p (h w) -> p h w", w=64)
                vector.tensor_copy(dst[:, :, 0:64], src)
                vector.memset(dst[:, :, 64:65], 1.0)
                p.inc(vector.drain(), "ev")

            block.vector(b_vev)
            vev_v = p.n["ev"]

            def b_vst(gp, grp=grp, jt=jt, uu=uu, vev_v=vev_v):
                p.w(gp, "ev", vev_v)
                dstp = uu % 4
                dst = io[f"vstage{grp}"][:, jt * 520:(jt + 1) * 520]
                p.inc(gp.dma_start(out=dst, in_=t["VBO"][:, dstp * 520:(dstp + 1) * 520]),
                      ctx["qkv_vs_sems"][uu % 4])

            block.gpsimd(b_vst)

        def b_ccv(gp, grp=grp, vals=[(nm, p.n[nm]) for nm in ctx["qkv_vs_sems"]]):
            for nm, v in vals:
                p.w(gp, nm, v)
            p.inc(gp.collective_compute(
                "AllGather", OP.bypass,
                replica_groups=[[0, 1], [2, 3], [4, 5], [6, 7]],
                ins=[io[f"vstage{grp}"].ap().opt()],
                outs=[io[f"cvo{grp}"].ap().opt()],
            ), "cc")

        block.gpsimd(b_ccv)
        ctx[f"cc_v{grp}"] = p.n["cc"]

    ctx["qkv_vs_end"] = [p.n[nm] for nm in ctx["qkv_vs_sems"]]


def ph_attn(p, io, ctx):
    t, block = p.t, p.block
    PSA, PSB, PSV, PSW = t["PSA"], t["PSB"], t["PSV"], t["PSW"]
    LNB = t["LNB"]

    ev0 = p.n["ev"]
    ex0 = p.n["ex"]
    unit_no = 0
    ex_cum = 0
    rg = 0
    dg = 0
    pv_rel = {}
    pl_rel = {}

    ctx["attn_ob0"] = [p.n["ob0"], p.n["ob1"]]
    cl_base = [p.n["cl0"], p.n["cl1"]]
    gi_base = [p.n["gi0"], p.n["gi1"]]

    def b_rect(gp):
        with gp.register("rtb") as rn:
            gp.reg_load(rn, t["RTBI"][0:1, 0:1])
            tb = gp.snap(rn, min_val=0, max_val=8)
            p.w(gp, "cc", ctx["cc_kq"])
            kq0 = io["ckqo"].ap()[0:1].squeeze(0)
            kv = kq0[:, 0:NPAIR * TC].rearrange(
                "p (t w) -> t p w", t=KT, w=1024)
            dk = io["rectkv"][:, 0:RVOFF].rearrange(
                "p (t w) -> t p w", t=8, w=1024)
            p.inc(gp.dma_start(out=dk, in_=kv[bass.ds(tb, 8)]), "dG")
            for grp in range(2):
                p.w(gp, "cc", ctx[f"cc_v{grp}"])
                if grp == 1:
                    p.w(gp, "dG", p.n["dG"])
                v0 = io[f"cvo{grp}"].ap()[0:1].squeeze(0)
                vv = v0.rearrange("p (t hw) -> t p hw", t=KT, hw=8 * 65)
                dv = io["rectkv"][:, RVOFF + grp * 4160:
                                  RVOFF + (grp + 1) * 4160].rearrange(
                    "p (t hw) -> t p hw", t=8, hw=8 * 65)
                p.inc(gp.dma_start(out=dv, in_=vv[bass.ds(tb, 8)]), "dG")

    block.gpsimd(b_rect)
    rect_dG = p.n["dG"]
    rect_dG_v0 = rect_dG - 16

    for sg in range(4):
        pairs = (2 * sg, 2 * sg + 1)

        def b_ld(sync, sg=sg, pairs=pairs):
            p.w(sync, "cc", ctx["cc_kq"])
            for i, nm in enumerate(ctx["qkv_vs_sems"]):
                p.w(sync, nm, ctx["qkv_vs_end"][i])
            p.w(sync, "dG", rect_dG_v0 if sg < 2 else rect_dG)
            if sg > 0:
                p.w(sync, "tE", ctx["attn_sg_t"][sg - 1])
            else:
                p.inc(sync.dma_start(out=t["DMSK"][:], in_=io["dmask_s"]), "dW")
            cc1r1 = io["ckqo"].ap()[1:2].squeeze(0)
            for i, pair in enumerate(pairs):
                p.inc(sync.dma_start(out=t["QG"][:, i * TC:(i + 1) * TC],
                                     in_=io["kqstage"][:, QOFF + pair * TC:
                                                       QOFF + (pair + 1) * TC]), "dW")
                ksrc = io["kqstage"][:, 0:NPAIR * TC].rearrange(
                    "p (t w) -> p t w", w=1024)[:, :, pair * 128:(pair + 1) * 128]
                kdst = t["KG"][:, i * TC:(i + 1) * TC].rearrange(
                    "p (t w) -> p t w", w=128)
                p.inc(sync.dma_start(out=kdst, in_=ksrc), "dW")
                p.inc(sync.dma_start(out=t["QR"][:, i * TC:(i + 1) * TC],
                                     in_=cc1r1[:, QOFF + pair * TC:
                                               QOFF + (pair + 1) * TC]), "dW")
                krsrc = io["rectkv"][:, 0:RVOFF].rearrange(
                    "p (t w) -> p t w", w=1024)[:, :, pair * 128:(pair + 1) * 128]
                krdst = t["KR"][:, i * 1024:(i + 1) * 1024].rearrange(
                    "p (t w) -> p t w", w=128)
                p.inc(sync.dma_start(out=krdst, in_=krsrc), "dW")
            g = sg // 2
            vfull = io[f"vstage{g}"][:, 0:KT * 8 * 65].rearrange(
                "p (t w) -> p t w", w=8 * 65)
            rvfull = io["rectkv"][:, RVOFF + g * 4160:RVOFF + (g + 1) * 4160].rearrange(
                "p (t w) -> p t w", w=8 * 65)
            for h4 in range(4):
                h = sg * 4 + h4
                src = vfull[:, :, (h % 8) * 65:((h % 8) + 1) * 65]
                dst = t["VG"][:, h4 * KT * 65:(h4 + 1) * KT * 65].rearrange(
                    "p (t w) -> p t w", w=65)
                p.inc(sync.dma_start(out=dst, in_=src), "dW")
                src2 = rvfull[:, :, (h % 8) * 65:((h % 8) + 1) * 65]
                dst2 = t["VR"][:, h4 * 8 * 65:(h4 + 1) * 8 * 65].rearrange(
                    "p (t w) -> p t w", w=65)
                p.inc(sync.dma_start(out=dst2, in_=src2), "dW")

        block.sync(b_ld)
        ld_dW = p.n["dW"]

        for pi in range(2):
            for qb in range(4):
                tiles = [("r", kt2, None) for kt2 in range(8)]
                ng = len(tiles)
                u = unit_no
                tE_u0 = p.n["tE"]
                ex_u0 = ex0 + ex_cum
                pv = PSV if u % 2 == 0 else PSW

                def b_tq(tensor, pi=pi, qb=qb, tiles=tiles, ng=ng, u=u,
                         tE_u0=tE_u0, ex_u0=ex_u0, pv=pv, ld_dW=ld_dW,
                         gt0=ex_cum):
                    p.w(tensor, "dW", ld_dW)
                    if u >= 2:
                        p.w(tensor, "ev", pv_rel[u - 2])
                    for gi, (kind, kt2, moff) in enumerate(tiles):
                        gt = gt0 + gi
                        if gt >= 2:
                            p.w(tensor, "ex", ex0 + gt - 1)
                        kcol = pi * 1024 + kt2 * 128
                        sbk = PSA if gi % 2 == 0 else PSB
                        tensor.matmul(sbk[:, 0:512], t["KR"][0:64, kcol:kcol + 128],
                                      t["QR"][0:64, pi * TC + qb * TTW:
                                              pi * TC + (qb + 1) * TTW],
                                      start=True, stop=True, tile_position=(0, 0))
                        tensor.matmul(sbk[:, 512:1024], t["KR"][64:128, kcol:kcol + 128],
                                      t["QR"][64:128, pi * TC + qb * TTW:
                                              pi * TC + (qb + 1) * TTW],
                                      start=True, stop=True, tile_position=(64, 0))
                        p.inc(tensor.drain(), "tE")
                        if gi >= 1:
                            _av(p, t, tensor, pi, tiles, gi - 1, ng, u, ex_u0, 0, pv, gt0)
                    _av(p, t, tensor, pi, tiles, ng - 1, ng, u, ex_u0, 0, pv, gt0)
                    p.inc(tensor.drain(), "tE")

                block.tensor(b_tq)

                def b_ex(scalar, tiles=tiles, ng=ng, tE_u0=tE_u0, gt0=ex_cum):
                    for gi in range(ng):
                        p.w(scalar, "tE", tE_u0 + gi + 1)
                        par = (gt0 + gi) % 4
                        sbk = PSA if gi % 2 == 0 else PSB
                        scalar.activation(t["PT"][:, par * 1024:(par + 1) * 1024],
                                          sbk[:, 0:1024], AF.Exp, bias=t["ZEROC"][:],
                                          scale=0.125)
                        p.inc(scalar.drain(), "ex")

                block.scalar(b_ex)
                ex_cum += ng

                def b_rp_ev(vector, rg=rg, u=u, tE_u0=tE_u0, ng=ng, pv=pv):
                    p.w(vector, "tE", tE_u0 + ng + 1)
                    if rg >= 2:
                        r2 = rg - 2
                        p.w(vector, f"cl{r2 % 2}",
                            cl_base[r2 % 2] + 16 * (r2 // 2 + 1))
                    slot = (rg % 2) * 1024
                    vector.tensor_copy(t["RP"][:, slot:slot + 1024],
                                       pv[:, 0:1024])
                    p.inc(vector.drain(), "ev")

                block.vector(b_rp_ev)
                pv_rel[u] = p.n["ev"]
                rpev = p.n["ev"]

                def b_rp_st(sync, sg=sg, rg=rg, pi=pi, qb=qb, rpev=rpev):
                    p.w(sync, "ev", rpev)
                    ru = pi * 4 + qb
                    slot = (rg % 2) * 1024
                    p.inc(sync.dma_start(
                        out=io[f"rectst{sg}"][:, ru * 1024:(ru + 1) * 1024],
                        in_=t["RP"][:, slot:slot + 1024]), f"cl{rg % 2}")

                block.sync(b_rp_st)
                rg += 1
                unit_no += 1

        def b_cc3(gp, sg=sg, q2=p.n["cl0"], q3=p.n["cl1"]):
            p.w(gp, "cl0", q2)
            p.w(gp, "cl1", q3)
            p.inc(gp.collective_compute(
                "AllGather", OP.bypass,
                replica_groups=[[0, 1], [2, 3], [4, 5], [6, 7]],
                ins=[io[f"rectst{sg}"].ap().opt()],
                outs=[io[f"cc3o{sg}"].ap().opt()],
            ), "cc")

        block.gpsimd(b_cc3)
        cc3_done = p.n["cc"]

        pend_zbnm = [None]
        for pi in range(2):
            for qb in range(4):
                tiles = [("d", tt2, None) for tt2 in range(qb * 4)]
                tiles += [("d", qb * 4 + i4, i4) for i4 in range(4)]
                ng = len(tiles)
                u = unit_no
                tE_u0 = p.n["tE"]
                ex_u0 = ex0 + ex_cum
                mk_u0 = p.n["mk"]
                pv = PSV if u % 2 == 0 else PSW

                def b_tq(tensor, pi=pi, qb=qb, tiles=tiles, ng=ng, u=u,
                         tE_u0=tE_u0, ex_u0=ex_u0, pv=pv, ld_dW=ld_dW,
                         gt0=ex_cum):
                    p.w(tensor, "dW", ld_dW)
                    if u >= 2:
                        p.w(tensor, "ev", pv_rel[u - 2])
                    for gi, (kind, kt2, moff) in enumerate(tiles):
                        gt = gt0 + gi
                        if gt >= 2:
                            p.w(tensor, "ex", ex0 + gt - 1)
                        kcol = pi * TC + kt2 * 128
                        sbk = PSA if gi % 2 == 0 else PSB
                        tensor.matmul(sbk[:, 0:512], t["KG"][0:64, kcol:kcol + 128],
                                      t["QG"][0:64, pi * TC + qb * TTW:
                                              pi * TC + (qb + 1) * TTW],
                                      start=True, stop=True, tile_position=(0, 0))
                        tensor.matmul(sbk[:, 512:1024], t["KG"][64:128, kcol:kcol + 128],
                                      t["QG"][64:128, pi * TC + qb * TTW:
                                              pi * TC + (qb + 1) * TTW],
                                      start=True, stop=True, tile_position=(64, 0))
                        p.inc(tensor.drain(), "tE")
                        if gi >= 1:
                            _av(p, t, tensor, pi, tiles, gi - 1, ng, u, ex_u0, mk_u0, pv, gt0)
                    _av(p, t, tensor, pi, tiles, ng - 1, ng, u, ex_u0, mk_u0, pv, gt0)
                    p.inc(tensor.drain(), "tE")

                block.tensor(b_tq)

                def b_ex(scalar, tiles=tiles, ng=ng, tE_u0=tE_u0, mk_u0=mk_u0,
                         gt0=ex_cum):
                    for gi, (kind, kt2, moff) in enumerate(tiles):
                        p.w(scalar, "tE", tE_u0 + gi + 1)
                        if gi >= 2 and tiles[gi - 2][2] is not None:
                            j = tiles[gi - 2][2]
                            p.w(scalar, "mk", mk_u0 + j + 1)
                        par = (gt0 + gi) % 4
                        sbk = PSA if gi % 2 == 0 else PSB
                        scalar.activation(t["PT"][:, par * 1024:(par + 1) * 1024],
                                          sbk[:, 0:1024], AF.Exp, bias=t["ZEROC"][:],
                                          scale=0.125)
                        p.inc(scalar.drain(), "ex")

                block.scalar(b_ex)
                ex_cum += ng

                def b_mk(vector, tiles=tiles, ex_u0=ex_u0, gt0=ex_cum):
                    for gi, (kind, kt2, moff) in enumerate(tiles):
                        if moff is None:
                            continue
                        p.w(vector, "ex", ex_u0 + gi + 1)
                        par = (gt0 + gi) % 4
                        for hh in range(2):
                            sl = slice(par * 1024 + hh * 512, par * 1024 + (hh + 1) * 512)
                            vector.tensor_mul(t["PT"][:, sl], t["PT"][:, sl],
                                              t["DMSK"][:, moff * 512:(moff + 1) * 512])
                        p.inc(vector.drain(), "mk")

                block.vector(b_mk)

                if pend_zbnm[0] is not None:
                    pend_zbnm[0]()
                    pend_zbnm[0] = None

                def b_pl(sync, sg=sg, dg=dg, pi=pi, qb=qb, cc3_done=cc3_done):
                    p.w(sync, "cc", cc3_done)
                    if dg >= 2:
                        p.w(sync, "ev", pl_rel[dg - 2])
                    du = pi * 4 + qb
                    slot = t["PL0"] if dg % 2 == 0 else t["PL1"]
                    for r in range(2):
                        src = io[f"cc3o{sg}"].ap()[r:r + 1].squeeze(0)
                        p.inc(sync.dma_start(
                            out=slot[0:65, r * 1024:(r + 1) * 1024],
                            in_=src[0:65, du * 1024:(du + 1) * 1024]), f"gi{dg % 2}")

                block.sync(b_pl)

                def b_padd(vector, dg=dg, tE_u0=tE_u0, ng=ng, pv=pv):
                    p.w(vector, "tE", tE_u0 + ng + 1)
                    p.w(vector, f"gi{dg % 2}", gi_base[dg % 2] + 32 * (dg // 2 + 1))
                    slot = t["PL0"] if dg % 2 == 0 else t["PL1"]
                    vector.tensor_add(slot[0:65, 0:1024], slot[0:65, 0:1024],
                                      slot[0:65, 1024:2048])
                    vector.drain()
                    vector.scalar_tensor_tensor(pv[0:65, 0:1024], slot[0:65, 0:1024],
                                                t["CMC"][0:65, :], pv[0:65, 0:1024],
                                                OP.mult, OP.add)
                    vector.drain()

                block.vector(b_padd)

                def b_ev(vector, pi=pi, qb=qb, sg=sg, ng=ng, tE_u0=tE_u0, pv=pv, dg=dg):
                    p.w(vector, "tE", tE_u0 + ng + 1)
                    if dg >= 2:
                        d2 = dg - 2
                        p.w(vector, f"ob{d2 % 2}", ctx["attn_ob0"][d2 % 2] + 16 * (d2 // 2 + 1))
                    pair = sg * 2 + pi
                    dst = LNB[:, pair * TC + qb * TTW: pair * TC + (qb + 1) * TTW]
                    vector.tensor_copy(dst[0:64, :], pv[0:64, 0:512])
                    vector.tensor_copy(t["OB"][0:64, (dg % 2) * 512:(dg % 2 + 1) * 512],
                                       pv[0:64, 512:1024])
                    vector.reciprocal(t["ZR"][64:65, 0:1024], pv[64:65, 0:1024])
                    p.inc(vector.drain(), "ev")

                block.vector(b_ev)
                ev_u = p.n["ev"]
                pl_rel[dg] = ev_u

                def b_obd(sync, pi=pi, qb=qb, sg=sg, dg=dg, ev_u=ev_u):
                    p.w(sync, "ev", ev_u)
                    pair = sg * 2 + pi
                    dst = LNB[:, pair * TC + qb * TTW: pair * TC + (qb + 1) * TTW]
                    p.inc(sync.dma_start(out=dst[64:128, :],
                                         in_=t["OB"][0:64, (dg % 2) * 512:(dg % 2 + 1) * 512]),
                          f"ob{dg % 2}")

                block.sync(b_obd)
                obd_u = (f"ob{dg % 2}", ctx["attn_ob0"][dg % 2] + 16 * (dg // 2 + 1))

                def emit_zbnm(pi=pi, qb=qb, sg=sg, pv=pv, ev_u=ev_u, obd_u=obd_u,
                              u=u):
                    def b_zb(tensor, pv=pv, ev_u=ev_u):
                        p.w(tensor, "ev", ev_u)
                        for hh in range(2):
                            tensor.matmul(pv[hh * 64:(hh + 1) * 64, 0:512],
                                          t["ONESF"][64:65, hh * 64:(hh + 1) * 64],
                                          t["ZR"][64:65, hh * 512:(hh + 1) * 512],
                                          start=True, stop=True,
                                          tile_position=(64, hh * 64))
                        p.inc(tensor.drain(), "tE")

                    block.tensor(b_zb)
                    zb_t = p.n["tE"]

                    def b_nm(vector, pi=pi, qb=qb, sg=sg, pv=pv, zb_t=zb_t,
                             obd_u=obd_u):
                        p.w(vector, "tE", zb_t)
                        p.w(vector, obd_u[0], obd_u[1])
                        pair = sg * 2 + pi
                        cs = slice(pair * TC + qb * TTW, pair * TC + (qb + 1) * TTW)
                        vector.tensor_mul(LNB[:, cs], LNB[:, cs], pv[:, 0:512])
                        p.inc(vector.drain(), "ev")

                    block.vector(b_nm)
                    pv_rel[u] = p.n["ev"]

                pend_zbnm[0] = emit_zbnm
                dg += 1
                unit_no += 1

        tight_t = p.n["tE"]
        if pend_zbnm[0] is not None:
            pend_zbnm[0]()
            pend_zbnm[0] = None
        ctx.setdefault("attn_sg_t", {})[sg] = tight_t
        ctx.setdefault("attn_sg_e", {})[sg] = p.n["ev"]

    def b_fence(vector):
        p.w(vector, "ev", p.n["ev"])
        p.inc(vector.drain(), "vE")

    block.vector(b_fence)
    ctx["attn_done_v"] = p.n["vE"]


def _av(p, t, tensor, pi, tiles, gi, ng, u, ex_u0, mk_u0, pv, gt0=0):
    kind, kt2, moff = tiles[gi]
    par = (gt0 + gi) % 4
    p.w(tensor, "ex", ex_u0 + gi + 1)
    if moff is not None:
        p.w(tensor, "mk", mk_u0 + moff + 1)
    Vs = t["VG"] if kind == "d" else t["VR"]
    hstride = (KT * 65) if kind == "d" else (8 * 65)
    for hh in range(2):
        h4 = pi * 2 + hh
        lhsT = Vs[:, h4 * hstride + kt2 * 65: h4 * hstride + (kt2 + 1) * 65]
        tensor.matmul(pv[0:65, hh * 512:(hh + 1) * 512], lhsT,
                      t["PT"][:, par * 1024 + hh * 512: par * 1024 + (hh + 1) * 512],
                      start=(gi == 0), stop=(gi == ng - 1))


def ph_outproj(p, io, ctx):
    t, block = p.t, p.block
    X, LNB = t["X"], t["LNB"]
    PSA, PSB = t["PSA"], t["PSB"]
    s0, wg0, _, cast_wb_op = stream_cast(p, io, ctx, "wo_s", 4, "op")
    a_done = ctx["attn_done_v"]
    ev0 = p.n["ev"]

    def b_mm(tensor):
        p.w(tensor, "vE", a_done)
        for oc in range(DC):
            piece = oc // 2
            par = piece % 2
            wcol0 = par * 2048 + (oc % 2) * 1024
            for tt in range(NTT):
                u = oc * NTT + tt
                p.w(tensor, "wb", cast_wb_op[piece])
                if u >= 4:
                    p.w(tensor, "ev", ev0 + u - 3)
                bank = (PSA, PSB)[(u % 4) // 2][:, (u % 2) * 512:(u % 2) * 512 + 512]
                for af in range(DC):
                    tensor.matmul(bank, t["WSB"][:, wcol0 + af * 128: wcol0 + af * 128 + 128],
                                  LNB[:, af * TC + tt * TTW: af * TC + (tt + 1) * TTW],
                                  start=(af == 0), stop=(af == DC - 1))
                p.inc(tensor.drain(), "tE")
                ctx.setdefault("op_mm_t", []).append(p.n["tE"])
            if oc % 2 == 1:
                p.inc(tensor.drain(), "wg")

    block.tensor(b_mm)

    def b_ev(vector):
        k = 0
        for oc in range(DC):
            for tt in range(NTT):
                p.w(vector, "tE", ctx["op_mm_t"][k]); k += 1
                u = oc * NTT + tt
                bank = (PSA, PSB)[(u % 4) // 2][:, (u % 2) * 512:(u % 2) * 512 + 512]
                cs = slice(oc * TC + tt * TTW, oc * TC + (tt + 1) * TTW)
                vector.tensor_add(X[:, cs], X[:, cs], bank)
                p.inc(vector.drain(), "ev")

    block.vector(b_ev)

    def b_fence(vector):
        p.w(vector, "ev", p.n["ev"])
        p.inc(vector.drain(), "vE")

    block.vector(b_fence)
    ctx["op_done_v"] = p.n["vE"]


def ph_hol(p, io, ctx, tag):
    t, block = p.t, p.block
    X, LNB = t["X"], t["LNB"]
    PSA, PSB = t["PSA"], t["PSB"]
    ln_done = ctx[("ln4" if tag == "hol1" else "ln6") + "_done_v"]
    wg0 = p.n["wg"]
    hb0 = [p.n["hb0"], p.n["hb1"]]

    def b_ld(sync):
        p.bar(sync, ["sE", "tE"])
        for j, (nm, val) in ctx["hol_scr_hw"].items():
            p.w(sync, nm, val)
        for oc in range(DC):
            par = oc % 2
            if oc >= 2:
                p.w(sync, "wg", wg0 + oc - 1)
            p.inc(sync.dma_start(out=t["WSB"][:, par * 2048: par * 2048 + 1024],
                                 in_=io["wscr_hol"][:, oc * 1024:(oc + 1) * 1024]), f"hb{par}")
            p.inc(sync.dma_start(out=t["WSB"][:, par * 2048 + 1024: par * 2048 + 2048],
                                 in_=io["wscr_hol"][:, 8192 + oc * 1024: 8192 + (oc + 1) * 1024]),
                  f"hb{par}")

    block.sync(b_ld)
    ev0 = p.n["ev"]

    PSV, PSW = t["PSV"], t["PSW"]

    def b_mm(tensor):
        p.w(tensor, "vE", ln_done)
        for g in range(DC * 2):
            oc, th = g // 2, g % 2
            par = oc % 2
            if th == 0:
                p.w(tensor, f"hb{par}", hb0[par] + 32 * (oc // 2 + 1))
            if g >= 2:
                p.w(tensor, "ev", ev0 + g - 1)
            rb = PSA if g % 2 == 0 else PSB
            ib = PSV if g % 2 == 0 else PSW
            for j in range(2):
                tt = th * 2 + j
                for kc in range(DC):
                    tensor.matmul(rb[:, j * 512:(j + 1) * 512],
                                  t["WSB"][:, par * 2048 + kc * 128: par * 2048 + kc * 128 + 128],
                                  LNB[:, kc * TC + tt * TTW: kc * TC + (tt + 1) * TTW],
                                  start=(kc == 0), stop=(kc == DC - 1))
                for kc in range(DC):
                    tensor.matmul(ib[:, j * 512:(j + 1) * 512],
                                  t["WSB"][:, par * 2048 + 1024 + kc * 128:
                                          par * 2048 + 1024 + kc * 128 + 128],
                                  LNB[:, kc * TC + tt * TTW: kc * TC + (tt + 1) * TTW],
                                  start=(kc == 0), stop=(kc == DC - 1))
            p.inc(tensor.drain(), "tE")
            ctx.setdefault(f"{tag}_mm_t", []).append(p.n["tE"])
            if th == 1:
                p.inc(tensor.drain(), "wg")

    block.tensor(b_mm)
    mm_ts = ctx[f"{tag}_mm_t"]

    def b_serial():
        for g in range(DC * 2):
            oc, th = g // 2, g % 2
            base = (g % 2) * 2048
            rb = PSA if g % 2 == 0 else PSB
            ib = PSV if g % 2 == 0 else PSW

            def s1(scalar, g=g, base=base, rb=rb, ib=ib):
                p.w(scalar, "tE", mm_ts[g])
                scalar.activation(t["WS"][:, base: base + 1024], rb[:, 0:1024], AF.Square)
                scalar.activation(t["WS"][:, base + 1024: base + 2048], ib[:, 0:1024],
                                  AF.Square)
                p.inc(scalar.drain(), "sE")

            block.scalar(s1)
            sv = p.n["sE"]

            def v1(vector, sv=sv, base=base):
                p.w(vector, "sE", sv)
                vector.tensor_add(t["WS"][:, base: base + 1024],
                                  t["WS"][:, base: base + 1024],
                                  t["WS"][:, base + 1024: base + 2048])
                p.inc(vector.drain(), "vE")

            block.vector(v1)
            vv = p.n["vE"]

            def s2(scalar, vv=vv, base=base):
                p.w(scalar, "vE", vv)
                scalar.activation(t["WS"][:, base + 1024: base + 2048],
                                  t["WS"][:, base: base + 1024], AF.Sqrt, scale=0.25)
                p.inc(scalar.drain(), "sE")

            block.scalar(s2)
            sv2 = p.n["sE"]

            def v2(vector, oc=oc, th=th, sv2=sv2, base=base):
                p.w(vector, "sE", sv2)
                cs = slice(oc * TC + th * 1024, oc * TC + (th + 1) * 1024)
                vector.tensor_add(X[:, cs], X[:, cs],
                                  t["WS"][:, base + 1024: base + 2048])
                p.inc(vector.drain(), "ev")

            block.vector(v2)

    b_serial()

    def b_fence(vector):
        p.w(vector, "ev", p.n["ev"])
        p.inc(vector.drain(), "vE")

    block.vector(b_fence)
    ctx[f"{tag}_done_v"] = p.n["vE"]


def ph_mixer(p, io, ctx):
    t, block = p.t, p.block
    X, LNB = t["X"], t["LNB"]
    PSA, PSB, PSW = t["PSA"], t["PSB"], t["PSW"]
    ln_done = ctx["ln5_done_v"]

    def rslot(c):
        return t["VALB"][:, c * 2052: c * 2052 + 2048]

    def b_tanh(scalar):
        p.bar(scalar, ["dW"])
        scalar.activation(t["ACOL"][:], t["APC"][:], AF.Tanh)
        p.inc(scalar.drain(), "sE")

    block.scalar(b_tanh)

    s0, wg0, mx_cast, mx_cast_sE = stream_cast(p, io, ctx, "win_s", 8, "mxw",
                                               defer_casts=True)
    mx_cast(0)
    mx_cast(1)
    ev0 = p.n["ev"]

    def b_mm(tensor):
        p.w(tensor, "vE", ln_done)
        for outc in range(16):
            piece = outc // 2
            par = piece % 2
            wcol0 = par * 2048 + (outc % 2) * 1024
            for tt in range(NTT):
                u = outc * NTT + tt
                p.w(tensor, "wb", mx_cast_sE[piece])
                if u >= 4:
                    p.w(tensor, "ev", ev0 + u - 3)
                bank = (PSA, PSB)[(u % 4) // 2][:, (u % 2) * 512:(u % 2) * 512 + 512]
                for kc in range(DC):
                    tensor.matmul(bank, t["WSB"][:, wcol0 + kc * 128: wcol0 + kc * 128 + 128],
                                  LNB[:, kc * TC + tt * TTW: kc * TC + (tt + 1) * TTW],
                                  start=(kc == 0), stop=(kc == DC - 1))
                p.inc(tensor.drain(), "tE")
                ctx.setdefault("mx_mm_t", []).append(p.n["tE"])
            if outc % 2 == 1:
                p.inc(tensor.drain(), "wg")

    block.tensor(b_mm)
    mm_ts = ctx["mx_mm_t"]

    gs0 = [p.n["gs0"], p.n["gs1"]]
    ev_pe0 = p.n["ev"]

    def pe_one(outc):
        def b_pe(scalar):
            for tt in range(NTT):
                u = outc * NTT + tt
                p.w(scalar, "tE", mm_ts[u])
                bank = (PSA, PSB)[(u % 4) // 2][:, (u % 2) * 512:(u % 2) * 512 + 512]
                if outc < 8:
                    ng = u
                    par = ng % 2
                    if ng >= 2:
                        n2 = ng - 2
                        p.w(scalar, f"gs{n2 % 2}", gs0[n2 % 2] + 16 * (n2 // 2 + 1))
                    scalar.activation(t["GIN"][:, par * TTW:(par + 1) * TTW], bank, AF.Sigmoid)
                else:
                    c = outc - 8
                    scalar.copy(t["VALB"][:, c * 2052 + 2 + tt * TTW: c * 2052 + 2 + (tt + 1) * TTW],
                                bank)
                p.inc(scalar.drain(), "ev")

        block.scalar(b_pe)

    for outc in range(16):
        if outc % 2 == 0 and 2 <= outc // 2 + 1 < 8:
            mx_cast(outc // 2 + 1)
        pe_one(outc)

    def b_gst(gp):
        ng = 0
        for outc in range(8):
            for tt in range(NTT):
                u = outc * NTT + tt
                p.w(gp, "ev", ev_pe0 + u + 1)
                par = ng % 2
                p.inc(gp.dma_start(
                    out=io["gstage"][:, outc * TC + tt * TTW: outc * TC + (tt + 1) * TTW],
                    in_=t["GIN"][:, par * TTW:(par + 1) * TTW]), f"gs{par}")
                ng += 1

    block.gpsimd(b_gst)
    win_ev = p.n["ev"]

    def b_edge(scalar):
        p.w(scalar, "ev", win_ev)
        for c in range(DC):
            scalar.copy(t["EDGE"][:, c * 4: c * 4 + 2], t["VALB"][:, c * 2052 + 2: c * 2052 + 4])
            scalar.copy(t["EDGE"][:, c * 4 + 2: c * 4 + 4],
                        t["VALB"][:, c * 2052 + 2048: c * 2052 + 2050])
        p.inc(scalar.drain(), "sE")

    block.scalar(b_edge)
    eg_s = p.n["sE"]

    def b_eg_d(sync):
        p.w(sync, "sE", eg_s)
        p.inc(sync.dma_start(out=io["h3a_in"][:, :], in_=t["EDGE"][:]), "dW")

    block.sync(b_eg_d)
    eg_dW = p.n["dW"]

    def b_cc3a(gp):
        p.w(gp, "dW", eg_dW)
        cc_v = p.inc(gp.collective_compute(
            "AllGather", OP.bypass,
            replica_groups=[[0, 1], [2, 3], [4, 5], [6, 7]],
            ins=[io["h3a_in"].ap().opt()], outs=[io["h3a_out"].ap().opt()]), "cc")
        p.w(gp, "cc", cc_v)
        with gp.register("rn3a") as rn:
            gp.reg_load(rn, t["NBRI"][0:1, 0:1])
            nbr = gp.snap(rn, min_val=0, max_val=1)
            p.inc(gp.dma_start(out=t["R3A"][:],
                               in_=io["h3a_out"].ap()[bass.ds(nbr, 1)].squeeze(0)), "dG")

    block.gpsimd(b_cc3a)
    r3a_dG = p.n["dG"]

    def b_halo(vector):
        p.w(vector, "dG", r3a_dG)
        for c in range(DC):
            vector.tensor_scalar(t["VALB"][:, c * 2052: c * 2052 + 2],
                                 t["R3A"][:, c * 4 + 2: c * 4 + 4], t["LMC"][:], None, OP.mult)
            vector.tensor_scalar(t["VALB"][:, c * 2052 + 2050: c * 2052 + 2052],
                                 t["R3A"][:, c * 4: c * 4 + 2], t["RMC"][:], None, OP.mult)
        p.inc(vector.drain(), "vE")

    block.vector(b_halo)
    halo_v = p.n["vE"]

    gv0 = p.n["mk"]
    cl0 = [p.n["cl0"], p.n["cl1"]]
    cvs0 = p.n["sE"]

    gi0 = [p.n["gi0"], p.n["gi1"]]
    gs_tot = [p.n["gs0"], p.n["gs1"]]

    for c in range(DC):
        def b_cvl(sync, c=c):
            if c == 0:
                p.bar(sync, ["sE", "tE"])
            if c >= 2:
                p.w(sync, "sE", cvs0 + c - 1)
            p.inc(sync.dma_start(out=t["WS"][:, (c % 2) * 2048:(c % 2) * 2048 + 768],
                                 in_=io["convd_s"][:, c * 768:(c + 1) * 768]), f"cl{c % 2}")

        block.sync(b_cvl)

        def b_cvc(scalar, c=c):
            p.w(scalar, f"cl{c % 2}", cl0[c % 2] + 16 * (c // 2 + 1))
            if c >= 2:
                p.w(scalar, "mk", gv0 + (c - 2) * (NTT + 1) + NTT + 1)
            scalar.copy(t["CVDS"][:, (c % 2) * 768:((c % 2) + 1) * 768],
                        t["WS"][:, (c % 2) * 2048:(c % 2) * 2048 + 768])
            p.inc(scalar.drain(), "sE")

        block.scalar(b_cvc)

        def b_gld(sync, c=c):
            if c == 0:
                for i in range(2):
                    p.w(sync, f"gs{i}", gs_tot[i])
            for tt in range(NTT):
                u = c * NTT + tt
                if u >= 2:
                    p.w(sync, "mk", gv0 + (u - 2) // NTT * (NTT + 1) + ((u - 2) % NTT) + 1)
                p.inc(sync.dma_start(
                    out=t["GIN"][:, (u % 2) * TTW:((u % 2) + 1) * TTW],
                    in_=io["gstage"][:, c * TC + tt * TTW: c * TC + (tt + 1) * TTW]),
                    f"gi{u % 2}")

        block.sync(b_gld)

    def b_conv(tensor):
        p.w(tensor, "vE", halo_v)
        for c in range(DC):
            p.w(tensor, "sE", cvs0 + c + 1)
            for tt in range(NTT):
                u = c * NTT + tt
                if u >= 4:
                    p.w(tensor, "mk", gv0 + (u - 4) // NTT * (NTT + 1) + ((u - 4) % NTT) + 1)
                bank = (PSA, PSB)[(u % 4) // 2][:, (u % 2) * 512:(u % 2) * 512 + 512]
                for k in range(5):
                    tensor.matmul(bank,
                                  t["CVDS"][:, (c % 2) * 768 + k * 128:(c % 2) * 768 + k * 128 + 128],
                                  t["VALB"][:, c * 2052 + tt * TTW + k:
                                           c * 2052 + tt * TTW + k + 512],
                                  start=(k == 0), stop=False)
                tensor.matmul(bank, t["CVDS"][:, (c % 2) * 768 + 5 * 128:(c % 2) * 768 + 6 * 128],
                              t["ONEB"][:], start=False, stop=True)
                p.inc(tensor.drain(), "tE")
                ctx.setdefault("mx_cv_t", []).append(p.n["tE"])

    block.tensor(b_conv)
    cv_ts = ctx["mx_cv_t"]

    def b_gv(vector):
        k = 0
        for c in range(DC):
            cpar = c % 2
            for tt in range(NTT):
                p.w(vector, "tE", cv_ts[k]); k += 1
                u = c * NTT + tt
                p.w(vector, f"gi{u % 2}", gi0[u % 2] + 16 * (u // 2 + 1))
                bank = (PSA, PSB)[(u % 4) // 2][:, (u % 2) * 512:(u % 2) * 512 + 512]
                vector.tensor_mul(t["GVB"][:, cpar * 512 + 0: cpar * 512 + 512] if False
                                  else t["GVB"][:, tt * TTW:(tt + 1) * TTW],
                                  t["GIN"][:, (u % 2) * TTW:((u % 2) + 1) * TTW], bank)
                p.inc(vector.drain(), "mk")
            vector.drain()
            a_bc = t["ACOL"][:, c:c + 1].broadcast_to([128, TC])
            vector.tensor_tensor_scan(t["YS"][:, c * TC:(c + 1) * TC], a_bc,
                                      t["GVB"][:, 0:TC],
                                      0.0, OP.mult, OP.add)
            p.inc(vector.drain(), "mk")

    block.vector(b_gv)
    scan_mk = p.n["mk"]

    def b_carr(vector):
        p.w(vector, "mk", scan_mk)
        for c in range(DC):
            vector.tensor_copy(t["CARR"][:, c:c + 1], t["YS"][:, c * TC + TC - 1: c * TC + TC])
        p.inc(vector.drain(), "vE")

    block.vector(b_carr)
    carr_v = p.n["vE"]

    def b_carr_d(sync):
        p.w(sync, "vE", carr_v)
        p.inc(sync.dma_start(out=io["h3b_in"][:, :], in_=t["CARR"][:]), "dW")

    block.sync(b_carr_d)
    carr_dW = p.n["dW"]

    def b_cc3b(gp):
        p.w(gp, "dW", carr_dW)
        cc_v = p.inc(gp.collective_compute(
            "AllGather", OP.bypass,
            replica_groups=[[0, 1], [2, 3], [4, 5], [6, 7]],
            ins=[io["h3b_in"].ap().opt()], outs=[io["h3b_out"].ap().opt()]), "cc")
        p.w(gp, "cc", cc_v)
        with gp.register("rn3b") as rn:
            gp.reg_load(rn, t["NBRI"][0:1, 0:1])
            nbr = gp.snap(rn, min_val=0, max_val=1)
            p.inc(gp.dma_start(out=t["R3B"][:],
                               in_=io["h3b_out"].ap()[bass.ds(nbr, 1)].squeeze(0)), "dG")

    block.gpsimd(b_cc3b)
    r3b_dG = p.n["dG"]

    def b_corr(vector):
        p.w(vector, "dG", r3b_dG)
        for c in range(DC):
            vector.tensor_scalar(t["CARR"][:, c:c + 1], t["R3B"][:, c:c + 1],
                                 t["CMC"][:], None, OP.mult)
        p.inc(vector.drain(), "vE")
        for c in range(DC):
            a_bc = t["ACOL"][:, c:c + 1].broadcast_to([128, TC])
            z_bc = t["ZEROC"][:].broadcast_to([128, TC])
            vector.tensor_tensor_scan(t["POWT"][:], a_bc, z_bc, 1.0, OP.mult, OP.add)
            vector.drain()
            vector.scalar_tensor_tensor(t["YS"][:, c * TC:(c + 1) * TC], t["POWT"][:],
                                        t["CARR"][:, c:c + 1],
                                        t["YS"][:, c * TC:(c + 1) * TC], OP.mult, OP.add)
            vector.drain()
        p.inc(vector.drain(), "vE")

    block.vector(b_corr)
    corr_v = p.n["vE"]

    s2, wg2, _, cast_wb_mxo = stream_cast(p, io, ctx, "wout_s", 4, "mxo")
    ev2 = p.n["ev"]

    def b_wo(tensor):
        p.w(tensor, "vE", corr_v)
        for oc in range(DC):
            piece = oc // 2
            par = piece % 2
            wcol0 = par * 2048 + (oc % 2) * 1024
            for tt in range(NTT):
                u = oc * NTT + tt
                p.w(tensor, "wb", cast_wb_mxo[piece])
                if u >= 4:
                    p.w(tensor, "ev", ev2 + u - 3)
                bank = (PSA, PSB)[(u % 4) // 2][:, (u % 2) * 512:(u % 2) * 512 + 512]
                for kc in range(DC):
                    tensor.matmul(bank, t["WSB"][:, wcol0 + kc * 128: wcol0 + kc * 128 + 128],
                                  t["YS"][:, kc * TC + tt * TTW: kc * TC + (tt + 1) * TTW],
                                  start=(kc == 0), stop=(kc == DC - 1))
                p.inc(tensor.drain(), "tE")
                ctx.setdefault("mx_wo_t", []).append(p.n["tE"])
            if oc % 2 == 1:
                p.inc(tensor.drain(), "wg")

    block.tensor(b_wo)
    wo_ts = ctx["mx_wo_t"]

    def b_res(vector):
        k = 0
        for oc in range(DC):
            for tt in range(NTT):
                p.w(vector, "tE", wo_ts[k]); k += 1
                u = oc * NTT + tt
                bank = (PSA, PSB)[(u % 4) // 2][:, (u % 2) * 512:(u % 2) * 512 + 512]
                vector.tensor_add(rslot(oc)[:, tt * TTW:(tt + 1) * TTW],
                                  LNB[:, oc * TC + tt * TTW: oc * TC + (tt + 1) * TTW], bank)
                p.inc(vector.drain(), "ev")

    block.vector(b_res)

    def b_fence(vector):
        p.w(vector, "ev", p.n["ev"])
        p.inc(vector.drain(), "vE")

    block.vector(b_fence)

    ph_ln(p, io, ctx, "ln7", rsl=rslot)
    ctx["mx_done_v"] = ctx["ln7_done_v"]


def ph_store(p, io, ctx):
    t, block = p.t, p.block

    def b_st(sync):
        p.bar(sync, ["vE", "tE", "sE", "ev"])
        for c in range(DC):
            p.inc(sync.dma_start(out=io["y_out"][c * 128:(c + 1) * 128, :],
                                 in_=t["X"][:, c * TC:(c + 1) * TC]), "dW")
        p.w(sync, "dW", p.n["dW"])

    block.sync(b_st)



def _host_layouts(inputs, parity):
    out = {}
    Wqkv = inputs["L0_attn_Wqkv"]
    Wq, Wk, Wv = Wqkv[0:D], Wqkv[D:2 * D], Wqkv[2 * D:3 * D]

    def pair_chunks(W):
        cols = []
        for pair in range(NPAIR):
            Wp = W[pair * 128:(pair + 1) * 128]
            for kc in range(DC):
                cols.append(Wp[:, kc * 128:(kc + 1) * 128].T)
        return np.concatenate(cols, axis=1)

    def v_chunks(W):
        cols = []
        for grp in range(2):
            Wg = W[grp * 512:(grp + 1) * 512]
            for kc in range(DC):
                cols.append(Wg[:, kc * 128:(kc + 1) * 128].T)
        return np.concatenate(cols, axis=1)

    out["wqkv_s"] = np.ascontiguousarray(
        np.concatenate([pair_chunks(Wq), pair_chunks(Wk), v_chunks(Wv)], axis=1),
        dtype=np.float32)

    Wo = inputs["L0_attn_Wo"]
    cols = []
    for oc in range(DC):
        for af in range(DC):
            cols.append(Wo[oc * 128:(oc + 1) * 128, af * 128:(af + 1) * 128].T)
    out["wo_s"] = np.ascontiguousarray(np.concatenate(cols, axis=1), dtype=np.float32)

    Win = inputs["L1_attn_W_in"]
    cols = []
    for outc in range(16):
        for kc in range(DC):
            cols.append(Win[outc * 128:(outc + 1) * 128, kc * 128:(kc + 1) * 128].T)
    out["win_s"] = np.ascontiguousarray(np.concatenate(cols, axis=1), dtype=np.float32)

    Wout = inputs["L1_attn_W_out"]
    cols = []
    for oc in range(DC):
        for kc in range(DC):
            cols.append(Wout[oc * 128:(oc + 1) * 128, kc * 128:(kc + 1) * 128].T)
    out["wout_s"] = np.ascontiguousarray(np.concatenate(cols, axis=1), dtype=np.float32)

    gw = inputs["L0_ffn_gate_w"]
    cols = [gw[:, kc * 128:(kc + 1) * 128].T for kc in range(DC)]
    out["gw_s"] = np.ascontiguousarray(np.concatenate(cols, axis=1), dtype=np.float32)
    out["gb_s"] = np.ascontiguousarray(inputs["L0_ffn_gate_b"].reshape(1, 8), dtype=np.float32)
    out["eb_s"] = np.ascontiguousarray(inputs["L0_ffn_expert_b"], dtype=np.float32)

    ew = inputs["L0_ffn_expert_w"]
    moe = np.zeros((128, 65536), dtype=np.float32)
    col = 0
    for half in range(2):
        for e in range(E):
            blk = np.empty((128, 4096), dtype=np.float32)
            for oc in range(4):
                o0 = (half * 4 + oc) * 128
                for kc in range(DC):
                    blk[:, (oc * DC + kc) * 128:(oc * DC + kc) * 128 + 128] = \
                        ew[e, o0:o0 + 128, kc * 128:(kc + 1) * 128].T
            moe[:, col:col + 4096] = blk
            col += 4096
    out["moew_s"] = moe

    def hol_stream(W):
        cols = []
        for oc in range(DC):
            for kc in range(DC):
                cols.append(W[kc * 128:(kc + 1) * 128, oc * 128:(oc + 1) * 128])
        return np.ascontiguousarray(np.concatenate(cols, axis=1), dtype=np.float32)

    out["theta_s"] = hol_stream(inputs["L1_ffn_theta"])
    out["thn_s"] = hol_stream(inputs["L1_ffn_theta_noise"])
    out["phi_s"] = hol_stream(inputs["L1_ffn_phi"])
    out["phn_s"] = hol_stream(inputs["L1_ffn_phi_noise"])

    cw = inputs["L1_attn_conv_w"][:, 0, :]
    cb = inputs["L1_attn_conv_b"]
    cvd = np.zeros((128, 6144), dtype=np.float32)
    for c in range(DC):
        for k in range(6):
            blk = np.zeros((128, 128), dtype=np.float32)
            src = cw[c * 128:(c + 1) * 128, k] if k < 5 else cb[c * 128:(c + 1) * 128]
            np.fill_diagonal(blk, src)
            cvd[:, (c * 6 + k) * 128:(c * 6 + k) * 128 + 128] = blk
    out["convd_s"] = cvd

    out["ap_s"] = np.ascontiguousarray(inputs["L1_attn_a_param"].reshape(DC, 128).T,
                                       dtype=np.float32)
    out["lnw_s"] = np.ascontiguousarray(inputs["L1_attn_ln_w"].reshape(DC, 128).T,
                                        dtype=np.float32)
    out["lnb_s"] = np.ascontiguousarray(inputs["L1_attn_ln_b"].reshape(DC, 128).T,
                                        dtype=np.float32)

    frq = np.exp(-math.log(10000.0) * np.arange(0, DH, 2, dtype=np.float64) / DH)
    frq_col = np.zeros((128, 1), dtype=np.float32)
    sgn_col = np.zeros((128, 1), dtype=np.float32)
    for pp in range(2):
        for d in range(DH):
            frq_col[pp * 64 + d, 0] = frq[d // 2]
            sgn_col[pp * 64 + d, 0] = -1.0 if d % 2 == 0 else 1.0
    out["frq_s"] = frq_col
    out["sgn_s"] = sgn_col

    pos = np.arange(parity * TC, (parity + 1) * TC, dtype=np.float64)
    ang = frq_col.astype(np.float64) * pos[None, :]
    out["rsc_s"] = np.sin(ang).astype(np.float32)
    out["rcc_s"] = np.cos(ang).astype(np.float32)

    dm = np.zeros((128, 2048), dtype=np.float32)
    kk = np.arange(128)[:, None]
    qq = np.arange(512)[None, :]
    for off in range(4):
        dm[:, off * 512:(off + 1) * 512] = (qq >= off * 128 + kk).astype(np.float32)
    out["dmask_s"] = dm
    out["eye_s"] = np.eye(128, dtype=np.float32)
    out["rbias_s"] = np.full((128, 1), MASK_NEG if parity == 0 else 0.0, dtype=np.float32)
    out["lmask_s"] = np.full((128, 1), 1.0 if parity == 1 else 0.0, dtype=np.float32)
    out["rmask_s"] = np.full((128, 1), 1.0 if parity == 0 else 0.0, dtype=np.float32)
    out["cmask_s"] = np.full((128, 1), 1.0 if parity == 1 else 0.0, dtype=np.float32)
    out["nbr_s"] = np.array([[1 - parity]], dtype=np.int32)
    out["rtb_s"] = np.array([[parity * 8]], dtype=np.int32)
    return out


def kernel(**inputs):
    inputs = {k: np.asarray(v) for k, v in inputs.items()}
    nc = build_program()
    lay = [_host_layouts(inputs, parity) for parity in range(2)]
    x = inputs["L0_attn_x"]
    in_maps = []
    for core in range(8):
        b, half = core // 2, core % 2
        m = dict(lay[half])
        m["x_in"] = np.ascontiguousarray(x[b, half * TC:(half + 1) * TC, :].T,
                                         dtype=np.float32)
        m["pos_s"] = np.arange(half * TC, (half + 1) * TC,
                               dtype=np.float32).reshape(1, TC)
        in_maps.append(m)
    res = run_bass_kernel_spmd(nc, in_maps, core_ids=list(range(8)))
    out = np.empty((B, T, D), dtype=np.float32)
    for core in range(8):
        b, half = core // 2, core % 2
        out[b, half * TC:(half + 1) * TC, :] = res.results[core]["y_out"].T
    return out


# version 17
# speedup vs baseline: 1.0078x; 1.0078x over previous
import math
import ml_dtypes
import numpy as np
from contextlib import ExitStack

import concourse.bass as bass
import concourse.mybir as mybir
from concourse.bass_utils import run_bass_kernel_spmd

F32 = mybir.dt.float32
BF16 = mybir.dt.bfloat16
I32 = mybir.dt.int32
OP = mybir.AluOpType
AF = mybir.ActivationFunctionType

B, T, D = 4, 4096, 1024
TC = 2048
DC = D // 128
TTW = 512
NTT = TC // TTW
H, DH = 16, 64
NPAIR = 8
E = 8
KT = TC // 128
LN_EPS = 1e-5
MASK_NEG = -60.0
VOFF = 2 * NPAIR * TC
_UPTO = None
_MOE_GATES_ONLY = False
QOFF = NPAIR * TC
KVW2 = 2 * NPAIR * TC + KT * H * 65
RVOFF = NPAIR * 1024
RKW = NPAIR * 1024 + 8 * H * 65


class P:

    def __init__(self, nc, block, est):
        self.nc, self.block, self.est = nc, block, est
        self.s = {}
        self.n = {}
        self.t = {}

    def sem(self, name):
        if name not in self.s:
            self.s[name] = self.est.enter_context(self.nc.semaphore(name))
            self.n[name] = 0
        return self.s[name]

    DMA_SEMS = ("dW", "dG", "ws", "lp", "wp", "hl", "hw", "wl", "qs", "vs",
                "ob", "gs", "gi", "hb", "cl")

    def inc(self, instr, name, k=None):
        if k is None:
            k = 16 if name.rstrip("0123") in self.DMA_SEMS else 1
        self.n[name] += k
        instr.then_inc(self.s[name], k)
        return self.n[name]

    def w(self, eng, name, val=None):
        eng.wait_ge(self.s[name], self.n[name] if val is None else val)

    def bar(self, eng, names):
        for nm in names:
            eng.wait_ge(self.s[nm], self.n[nm])


SEMS = (["dW", "dG", "tE", "vE", "sE", "cc", "wb", "xe", "ex", "mk", "ev", "wg", "wg2", "gB"]
        + [f"ws{i}" for i in range(2)] + [f"lp{i}" for i in range(2)] + [f"wp{i}" for i in range(2)]
        + [f"hl{i}" for i in range(2)] + [f"hw{i}" for i in range(2)] + [f"wl{i}" for i in range(2)]
        + [f"qs{i}" for i in range(4)] + [f"vs{i}" for i in range(2)] + [f"ob{i}" for i in range(2)]
        + [f"gs{i}" for i in range(2)] + [f"gi{i}" for i in range(2)] + [f"hb{i}" for i in range(2)]
        + [f"cl{i}" for i in range(2)])


def build_program():
    nc = bass.Bass("TRN2", target_bir_lowering=False)

    def din(name, shape, dt=F32):
        return nc.dram_tensor(name, shape, dt, kind="ExternalInput").ap()

    io = {}
    io["x_in"] = din("x_in", [D, TC])
    io["wqkv_s"] = din("wqkv_s", [128, 24576])
    io["wo_s"] = din("wo_s", [128, 8192])
    io["win_s"] = din("win_s", [128, 16384])
    io["wout_s"] = din("wout_s", [128, 8192])
    io["gw_s"] = din("gw_s", [128, 64])
    io["gb_s"] = din("gb_s", [1, 8])
    io["eb_s"] = din("eb_s", [8, 1024])
    io["moew_s"] = din("moew_s", [128, 65536])
    io["theta_s"] = din("theta_s", [128, 8192])
    io["thn_s"] = din("thn_s", [128, 8192])
    io["phi_s"] = din("phi_s", [128, 8192])
    io["phn_s"] = din("phn_s", [128, 8192])
    io["convd_s"] = din("convd_s", [128, 6144])
    io["ap_s"] = din("ap_s", [128, 8])
    io["lnw_s"] = din("lnw_s", [128, 8])
    io["lnb_s"] = din("lnb_s", [128, 8])
    io["pos_s"] = din("pos_s", [1, TC])
    io["frq_s"] = din("frq_s", [128, 1])
    io["sgn_s"] = din("sgn_s", [128, 1])
    io["rsc_s"] = din("rsc_s", [128, TC])
    io["rcc_s"] = din("rcc_s", [128, TC])
    io["rbias_s"] = din("rbias_s", [128, 1])
    io["lmask_s"] = din("lmask_s", [128, 1])
    io["rmask_s"] = din("rmask_s", [128, 1])
    io["cmask_s"] = din("cmask_s", [128, 1])
    io["nbr_s"] = din("nbr_s", [1, 1], I32)
    io["rtb_s"] = din("rtb_s", [1, 1], I32)
    io["dmask_s"] = din("dmask_s", [128, 2048])
    io["eye_s"] = din("eye_s", [128, 128])
    io["y_out"] = nc.dram_tensor("y_out", [D, TC], F32, kind="ExternalOutput").ap()

    io["wscr_moe"] = nc.dram_tensor("wscr_moe", [128, 65536], BF16, kind="Internal")
    io["wscr_hol"] = nc.dram_tensor("wscr_hol", [128, 16384], BF16, kind="Internal")
    io["kqstage"] = nc.dram_tensor("kqstage", [128, 2 * NPAIR * TC], BF16, kind="Internal")
    io["vstage0"] = nc.dram_tensor("vstage0", [128, KT * 8 * 65], BF16, kind="Internal")
    io["vstage1"] = nc.dram_tensor("vstage1", [128, KT * 8 * 65], BF16, kind="Internal")
    io["ckqo"] = nc.dram_tensor("ckqo", [2, 128, 2 * NPAIR * TC], BF16, kind="Internal")
    io["cvo0"] = nc.dram_tensor("cvo0", [2, 128, KT * 8 * 65], BF16, kind="Internal")
    io["cvo1"] = nc.dram_tensor("cvo1", [2, 128, KT * 8 * 65], BF16, kind="Internal")
    io["rectkv"] = nc.dram_tensor("rectkv", [128, RKW], BF16, kind="Internal")
    for _sg in range(4):
        io[f"rectst{_sg}"] = nc.dram_tensor(f"rectst{_sg}", [128, 8192], BF16,
                                            kind="Internal")
        io[f"cc3o{_sg}"] = nc.dram_tensor(f"cc3o{_sg}", [2, 128, 8192], BF16,
                                          kind="Internal")
    io["gstage"] = nc.dram_tensor("gstage", [128, DC * TC], BF16, kind="Internal")
    io["wmskd"] = nc.dram_tensor("wmskd", [8, TC], BF16, kind="Internal")
    io["h3a_in"] = nc.dram_tensor("h3a_in", [128, 32], F32, kind="Internal")
    io["h3a_out"] = nc.dram_tensor("h3a_out", [2, 128, 32], F32, kind="Internal")
    io["h3b_in"] = nc.dram_tensor("h3b_in", [128, 8], F32, kind="Internal")
    io["h3b_out"] = nc.dram_tensor("h3b_out", [2, 128, 8], F32, kind="Internal")

    est = ExitStack()
    p = P(nc, None, est)
    t = p.t
    sb = lambda name, shape, dt: t.__setitem__(name, est.enter_context(nc.sbuf_tensor(name, shape, dt)))
    psm = lambda name, shape: t.__setitem__(name, est.enter_context(nc.psum_tensor(name, shape, F32)))

    sb("X", [128, DC * TC], F32)
    sb("LNB", [128, DC * TC], BF16)
    sb("AR", [128, 35904], BF16)
    sb("EBB", [8, 1024], BF16)
    sb("WS", [128, 2 * 2048], F32)
    sb("WSB", [128, 2 * 2048], BF16)
    sb("ROWS", [1, 2 * TTW], F32)
    sb("XSQ", [128, TTW], BF16)
    sb("TF", [128, TTW], F32)
    sb("TF2", [128, TTW], F32)
    sb("SMF", [128, 256], F32)
    sb("SMB", [128, 2304], BF16)
    sb("SM1F", [1, 144], F32)
    sb("SM1B", [1, 144], BF16)
    sb("NBRI", [1, 1], I32)
    sb("RTBI", [1, 1], I32)
    psm("PSA", [128, 1024])
    psm("PSB", [128, 1024])
    psm("PSV", [128, 1024])
    psm("PSW", [128, 1024])

    t["FRQ"] = t["SMF"][:, 0:1]
    t["SGN"] = t["SMF"][:, 1:2]
    t["RBIAS"] = t["SMF"][:, 2:3]
    t["ZEROC"] = t["SMF"][:, 3:4]
    t["LMC"] = t["SMF"][:, 4:5]
    t["RMC"] = t["SMF"][:, 5:6]
    t["CMC"] = t["SMF"][:, 6:7]
    t["EPSC"] = t["SMF"][0:1, 7:8]
    t["HPI"] = t["SMF"][:, 8:9]
    t["ACOL"] = t["SMF"][:, 16:24]
    t["APC"] = t["SMF"][:, 24:32]
    t["LNWC"] = t["SMF"][:, 32:40]
    t["LNBC"] = t["SMF"][:, 40:48]
    t["CARR"] = t["SMF"][:, 48:56]
    t["R3B"] = t["SMF"][:, 56:64]
    t["EDGE"] = t["SMF"][:, 64:96]
    t["R3A"] = t["SMF"][:, 96:128]
    t["ONESF"] = t["SMF"][:, 128:256]
    t["EYEB"] = t["SMB"][:, 0:128]
    t["ONEB"] = t["SMB"][:, 128:640]
    t["WTM"] = t["SMB"][:, 640:656]
    t["GWB"] = t["SMB"][:, 656:720]
    t["CVDS"] = t["SMB"][:, 768:2304]
    t["ONEC"] = t["SM1F"][:, 0:128]
    t["GBF"] = t["SM1F"][:, 128:136]
    t["ONER"] = t["SM1B"][:, 0:128]
    t["GBB"] = t["SM1B"][:, 128:136]

    AR = t["AR"]
    ab = lambda off, n: AR[:, off:off + n]
    af = lambda off, n: AR[:, off:off + 2 * n].bitcast(F32)
    t["RSC"] = af(0, 2048)
    t["RSC2"] = af(4096, 2048)
    t["QRAW"] = af(8192, 1024)
    t["POSR"] = af(10240, 2048)[0:1, :]
    t["BNC"] = ab(14336, 2048)
    t["VBO"] = ab(16384, 2080)
    t["KG"] = ab(0, 4096)
    t["QG"] = ab(4096, 4096)
    t["KR"] = ab(8192, 4096)
    t["VG"] = ab(12288, 4160)
    t["VR"] = ab(16448, 4160)
    t["PT"] = ab(20608, 4096)
    t["ZR"] = af(24704, 1024)
    t["DMSK"] = t["WS"][:, 0:2048]
    t["OB"] = ab(26752, 1024)
    t["QR"] = ab(27776, 4096)
    t["RP"] = ab(31872, 2048)
    t["PL0"] = ab(8192 + 2048, 2048)
    t["PL1"] = ab(16448 + 2080, 2048)
    t["XE"] = ab(0, 8192)
    t["WBC"] = ab(8192, 4096)
    t["WMSK"] = ab(12288, 2048)[0:8, :]
    t["WG"] = ab(14336, 8192)
    t["WROW"] = af(22528, 2048)[0:8, :]
    t["VALB"] = ab(0, 16416)
    t["YS"] = ab(16448, 16384)
    t["GVB"] = ab(32832, 2048)
    t["GIN"] = ab(34880, 1024)
    t["POWT"] = ab(32832, 2048)

    for s in SEMS:
        p.sem(s)

    phases = [
        ("init", lambda c: ph_init(p, io, c)),
        ("ln1", lambda c: ph_ln(p, io, c, "ln1")),
        ("moe1g", lambda c: ph_moe_gates(p, io, c, "moe1")),
        ("prep", lambda c: ph_prep(p, io, c)),
        ("moe1", lambda c: ph_moe(p, io, c, "moe1")),
        ("ln2", lambda c: ph_ln(p, io, c, "ln2")),
        ("qkv", lambda c: ph_qkv(p, io, c)),
        ("attn", lambda c: ph_attn(p, io, c)),
        ("outproj", lambda c: ph_outproj(p, io, c)),
        ("ln3", lambda c: ph_ln(p, io, c, "ln3")),
        ("moe2", lambda c: ph_moe(p, io, c, "moe2")),
        ("ln4", lambda c: ph_ln(p, io, c, "ln4")),
        ("hol1", lambda c: ph_hol(p, io, c, "hol1")),
        ("ln5", lambda c: ph_ln(p, io, c, "ln5")),
        ("mixer", lambda c: ph_mixer(p, io, c)),
        ("ln6", lambda c: ph_ln(p, io, c, "ln6")),
        ("hol2", lambda c: ph_hol(p, io, c, "hol2")),
    ]
    with est:
        with nc.Block() as block:
            p.block = block
            ctx = {}
            for name, fn in phases:
                fn(ctx)
                if _UPTO is not None and name == _UPTO:
                    break
            ph_store(p, io, ctx)
    return nc


def stream_cast(p, io, ctx, src_name, npieces, tag, src_off=0, wait_dW=None,
                extra_waits=(), defer_casts=False):
    t = p.t
    wg0 = p.n["wg"]
    wsb = [p.n["ws0"], p.n["ws1"]]

    def b_sync(sync):
        p.bar(sync, ["sE", "tE", "vE"])
        if wait_dW is not None:
            p.w(sync, "dW", wait_dW)
        for i in range(npieces):
            par = i % 2
            if i >= 2:
                p.w(sync, "wg", wg0 + i - 1)
            p.inc(sync.dma_start(out=t["WS"][:, par * 2048:(par + 1) * 2048],
                                 in_=io[src_name][:, src_off + i * 2048: src_off + (i + 1) * 2048]),
                  f"ws{par}")

    p.block.sync(b_sync)
    s0 = p.n["sE"]
    wb0 = p.n["wb"]
    cast_wb = {i: wb0 + i + 1 for i in range(npieces)}

    def cast_piece(i):
        def b_one(scalar):
            par = i % 2
            if i < 2:
                for nm, val in extra_waits:
                    p.w(scalar, nm, val)
            p.w(scalar, f"ws{par}", wsb[par] + 16 * (i // 2 + 1))
            if i >= 2:
                p.w(scalar, "wg", wg0 + i - 1)
            scalar.copy(t["WSB"][:, par * 2048:(par + 1) * 2048],
                        t["WS"][:, par * 2048:(par + 1) * 2048])
            p.inc(scalar.drain(), "wb")

        p.block.scalar(b_one)

    if not defer_casts:
        for i in range(npieces):
            cast_piece(i)
    ctx[f"{tag}_cast0"] = s0
    ctx[f"{tag}_wg0"] = wg0
    return s0, wg0, cast_piece, cast_wb


def ph_init(p, io, ctx):
    t, block = p.t, p.block

    def b_sync(sync):
        for c in range(DC):
            p.inc(sync.dma_start(out=t["X"][:, c * TC:(c + 1) * TC],
                                 in_=io["x_in"][c * 128:(c + 1) * 128, :]), "dW")
        for dst, src in [("FRQ", "frq_s"), ("SGN", "sgn_s"), ("RBIAS", "rbias_s"),
                         ("LMC", "lmask_s"), ("RMC", "rmask_s"), ("CMC", "cmask_s"),
                         ("APC", "ap_s"), ("LNWC", "lnw_s"), ("LNBC", "lnb_s"),
                         ("NBRI", "nbr_s"), ("RTBI", "rtb_s"), ("GBF", "gb_s")]:
            p.inc(sync.dma_start(out=t[dst][:], in_=io[src]), "dW")
        p.inc(sync.dma_start(out=t["WS"][:, 0:64], in_=io["gw_s"]), "dW")
        p.inc(sync.dma_start(out=t["WS"][0:8, 64:1088], in_=io["eb_s"]), "dW")
        p.inc(sync.dma_start(out=t["TF"][:, 0:128], in_=io["eye_s"]), "dW")

    block.sync(b_sync)
    ld0 = p.n["dW"]

    def b_scalar(scalar):
        p.w(scalar, "dW", ld0)
        scalar.copy(t["GWB"][:], t["WS"][:, 0:64])
        scalar.copy(t["EBB"][:], t["WS"][0:8, 64:1088])
        scalar.copy(t["EYEB"][:], t["TF"][:, 0:128])
        scalar.copy(t["GBB"][:], t["GBF"][:])
        p.inc(scalar.drain(), "sE")

    block.scalar(b_scalar)
    ctx["init_s"] = p.n["sE"]

    def b_vector(vector):
        for name, val in [("ONEC", 1.0), ("ONER", 1.0), ("ONEB", 1.0), ("ZEROC", 0.0),
                          ("EPSC", LN_EPS), ("HPI", float(np.pi / 2))]:
            vector.memset(t[name][:], val)
        vector.memset(t["ONESF"][:], 1.0)
        p.inc(vector.drain(), "vE")

    block.vector(b_vector)
    ctx["init_v"] = p.n["vE"]


def ph_prep(p, io, ctx):
    t, block = p.t, p.block

    s0 = p.n["sE"]
    lpb = [p.n["lp0"], p.n["lp1"]]
    wpb = [p.n["wp0"], p.n["wp1"]]

    def _prep_q(eng, par):
        if par == 0:
            p.w(eng, "sE", ctx["init_s"])
        for i in range(par, 32, 2):
            if i >= 2:
                p.w(eng, "sE", s0 + i - 1)
            p.inc(eng.dma_start(out=t["WS"][:, par * 2048:(par + 1) * 2048],
                                in_=io["moew_s"][:, i * 2048:(i + 1) * 2048]), f"lp{par}")
            if i >= 2:
                j = i - 2
                p.w(eng, "sE", s0 + j + 1)
                p.inc(eng.dma_start(out=io["wscr_moe"][:, j * 2048:(j + 1) * 2048],
                                    in_=t["WSB"][:, par * 2048:(par + 1) * 2048]),
                      f"wp{par}")
        j = 30 + par
        p.w(eng, "sE", s0 + j + 1)
        p.inc(eng.dma_start(out=io["wscr_moe"][:, j * 2048:(j + 1) * 2048],
                            in_=t["WSB"][:, par * 2048:(par + 1) * 2048]),
              f"wp{par}")

    def b_sync2(sync):
        _prep_q(sync, 0)

    def b_gp2(gp):
        p.w(gp, "sE", ctx["init_s"])
        _prep_q(gp, 1)

    def b_scalar2(scalar):
        for i in range(32):
            par = i % 2
            p.w(scalar, f"lp{par}", lpb[par] + 16 * (i // 2 + 1))
            if i >= 2:
                j = i - 2
                p.w(scalar, f"wp{j % 2}", wpb[j % 2] + 16 * (j // 2 + 1))
            scalar.copy(t["WSB"][:, par * 2048:(par + 1) * 2048],
                        t["WS"][:, par * 2048:(par + 1) * 2048])
            p.inc(scalar.drain(), "sE")

    block.sync(b_sync2)
    block.gpsimd(b_gp2)
    block.scalar(b_scalar2)
    ctx["moe_scr_wb"] = {j: wpb[j % 2] + 16 * (j // 2 + 1) for j in range(32)}
    ctx["moe_scr_sem"] = {j: f"wp{j % 2}" for j in range(32)}

    hv0 = p.n["vE"]
    hs0 = p.n["sE"]
    hl0 = p.n["hl0"]
    hwb = [p.n["hw0"], p.n["hw1"]]

    def b_sync4(sync):
        p.w(sync, "sE", hs0)
        for k in range(8):
            w, i = divmod(k, 4)
            a_s = io["theta_s"] if w == 0 else io["phi_s"]
            n_s = io["thn_s"] if w == 0 else io["phn_s"]
            if k >= 1:
                p.w(sync, "sE", hs0 + k)
            p.inc(sync.dma_start(out=t["WS"][:, 0:2048], in_=a_s[:, i * 2048:(i + 1) * 2048]), "hl0")
            p.inc(sync.dma_start(out=t["WS"][:, 2048:4096], in_=n_s[:, i * 2048:(i + 1) * 2048]), "hl0")
            if k >= 2:
                j = k - 2
                p.w(sync, "sE", hs0 + j + 1)
                p.inc(sync.dma_start(out=io["wscr_hol"][:, j * 2048:(j + 1) * 2048],
                                     in_=t["WSB"][:, (j % 2) * 2048:((j % 2) + 1) * 2048]),
                      f"hw{j % 2}")
        for j in (6, 7):
            p.w(sync, "sE", hs0 + j + 1)
            p.inc(sync.dma_start(out=io["wscr_hol"][:, j * 2048:(j + 1) * 2048],
                                 in_=t["WSB"][:, (j % 2) * 2048:((j % 2) + 1) * 2048]),
                  f"hw{j % 2}")

    def b_vector2(vector):
        for k in range(8):
            p.w(vector, "hl0", hl0 + 32 * (k + 1))
            if k >= 1:
                p.w(vector, "sE", hs0 + k)
            vector.tensor_add(t["WS"][:, 0:2048], t["WS"][:, 0:2048], t["WS"][:, 2048:4096])
            p.inc(vector.drain(), "vE")

    def b_scalar3(scalar):
        for k in range(8):
            w, i = divmod(k, 4)
            p.w(scalar, "vE", hv0 + k + 1)
            if k < 2:
                p.w(scalar, ctx["moe_scr_sem"][30 + k], ctx["moe_scr_wb"][30 + k])
            if k >= 2:
                j = k - 2
                p.w(scalar, f"hw{j % 2}", hwb[j % 2] + 16 * (j // 2 + 1))
            par = k % 2
            if w == 0:
                scalar.activation(t["WSB"][:, par * 2048:(par + 1) * 2048],
                                  t["WS"][:, 0:2048], AF.Sin, bias=t["HPI"][:], scale=1.0)
            else:
                scalar.activation(t["WSB"][:, par * 2048:(par + 1) * 2048],
                                  t["WS"][:, 0:2048], AF.Sin)
            p.inc(scalar.drain(), "sE")

    block.sync(b_sync4)
    block.vector(b_vector2)
    block.scalar(b_scalar3)
    ctx["hol_scr_hw"] = {j: (f"hw{j % 2}", hwb[j % 2] + 16 * (j // 2 + 1)) for j in range(8)}
    ctx["prep_done_dW"] = p.n["dW"]
    ctx["prep_done_sE"] = p.n["sE"]


def ph_ln(p, io, ctx, tag, affine=False, rsl=None, extra_residual=False):
    t, block = p.t, p.block
    X, LNB = t["X"], t["LNB"]
    stats = t["PSW"][:, 0:512]

    if rsl is None:
        def b_cast(scalar):
            p.bar(scalar, ["vE", "tE"])
            for c in range(DC):
                scalar.copy(LNB[:, c * TC:(c + 1) * TC], X[:, c * TC:(c + 1) * TC])
            p.inc(scalar.drain(), "sE")

        block.scalar(b_cast)

    for tt in range(NTT):
        tok = slice(tt * TTW, (tt + 1) * TTW)
        src = (lambda c: LNB[:, c * TC + tt * TTW: c * TC + (tt + 1) * TTW]) if rsl is None \
            else (lambda c: rsl(c)[:, tok])
        xf = (lambda c: X[:, c * TC + tt * TTW: c * TC + (tt + 1) * TTW]) if rsl is None \
            else (lambda c: rsl(c)[:, tok])

        fast = rsl is None or tag == "ln7"
        sqd = t["KG"] if tag == "ln1" else t["WSB"]
        if fast:
            def b_sq(scalar, src=src, tt=tt, sqd=sqd):
                if tt == 0:
                    p.bar(scalar, ["sE", "tE", "vE"])
                    gates = {}
                    for nm, val in ctx.get("hol_scr_hw", {}).values():
                        gates[nm] = max(gates.get(nm, 0), val)
                    for j, val in ctx.get("moe_scr_wb", {}).items():
                        nm = ctx["moe_scr_sem"][j]
                        gates[nm] = max(gates.get(nm, 0), val)
                    for nm, val in gates.items():
                        p.w(scalar, nm, val)
                else:
                    p.w(scalar, "tE", ctx[f"{tag}_stats_t"][tt - 1])
                for c in range(DC):
                    scalar.activation(sqd[:, c * TTW:(c + 1) * TTW], src(c),
                                      AF.Square)
                p.inc(scalar.drain(), "sE")

            block.scalar(b_sq)
            sqv = p.n["sE"]

            def b_stats(tensor, src=src, sqv=sqv, sqd=sqd):
                p.w(tensor, "sE", sqv)
                for c in range(DC):
                    tensor.matmul(stats[0:1, :], t["ONEB"][:, 0:1], src(c),
                                  start=(c == 0), stop=(c == DC - 1))
                    tensor.matmul(t["PSW"][0:1, 512:1024], t["ONEB"][:, 0:1],
                                  sqd[:, c * TTW:(c + 1) * TTW],
                                  start=(c == 0), stop=(c == DC - 1))
                p.inc(tensor.drain(), "tE")

            block.tensor(b_stats)
            ctx.setdefault(f"{tag}_stats_t", {})[tt] = p.n["tE"]
        else:
            sq0 = p.n["sE"]

            def b_sq(scalar, src=src):
                p.bar(scalar, ["sE", "tE", "vE"])
                for c in range(DC):
                    if c >= 1:
                        p.w(scalar, "tE", ctx[f"{tag}_st0"] + c)
                    scalar.activation(t["XSQ"][:, :], src(c), AF.Square)
                    p.inc(scalar.drain(), "sE")

            ctx[f"{tag}_st0"] = p.n["tE"]
            block.scalar(b_sq)

            def b_stats(tensor, src=src, sq0=sq0):
                for c in range(DC):
                    p.w(tensor, "sE", sq0 + c + 1)
                    tensor.matmul(stats[0:1, :], t["ONEB"][:, 0:1], src(c),
                                  start=(c == 0), stop=(c == DC - 1))
                    tensor.matmul(t["PSW"][0:1, 512:1024], t["ONEB"][:, 0:1], t["XSQ"][:, :],
                                  start=(c == 0), stop=(c == DC - 1))
                    p.inc(tensor.drain(), "tE")

            block.tensor(b_stats)

        X0 = t["ROWS"][0:1, 0:512]
        X1 = t["ROWS"][0:1, 512:1024]

        msq = (t["PSV"] if fast else t["PSA"])[0:1, 0:512]

        def b_r1(scalar, msq=msq):
            p.bar(scalar, ["tE"])
            scalar.activation(X0, stats[0:1, :], AF.Identity, scale=1.0 / D)
            scalar.activation(X1, t["PSW"][0:1, 512:1024], AF.Identity, scale=1.0 / D)
            scalar.drain()
            scalar.activation(msq, X0, AF.Square)
            p.inc(scalar.drain(), "sE")

        block.scalar(b_r1)

        def b_r3(vector, msq=msq):
            p.bar(vector, ["sE"])
            vector.scalar_tensor_tensor(X1, msq, -1.0, X1,
                                        OP.mult, OP.add)
            p.inc(vector.drain(), "vE")

        block.vector(b_r3)

        def b_r4(scalar):
            p.bar(scalar, ["vE"])
            scalar.activation(X1, X1, AF.Sqrt, bias=t["EPSC"][:], scale=1.0)
            p.inc(scalar.drain(), "sE")

        block.scalar(b_r4)

        def b_r5(vector):
            p.bar(vector, ["sE"])
            vector.reciprocal(X1, X1)
            vector.drain()
            vector.scalar_tensor_tensor(X0, X0, -1.0, X1, OP.mult, OP.mult)
            p.inc(vector.drain(), "vE")

        block.vector(b_r5)

        def b_bc(tensor):
            p.bar(tensor, ["vE"])
            tensor.matmul(t["PSA"][:, 0:512], t["ONESF"][0:1, 0:128], X1,
                          start=True, stop=True)
            tensor.matmul(t["PSB"][:, 0:512], t["ONESF"][0:1, 0:128], X0,
                          start=True, stop=True)
            p.inc(tensor.drain(), "tE")

        block.tensor(b_bc)

        def b_nm(vector, src=src, xf=xf):
            p.bar(vector, ["tE"])
            if rsl is None:
                for c in range(0, DC, 2):
                    tfa = t["TF"][:, :]
                    tfb = t["TF2"][:, 0:TTW]
                    csa = slice(c * TC + tt * TTW, c * TC + (tt + 1) * TTW)
                    csb = slice((c + 1) * TC + tt * TTW, (c + 1) * TC + (tt + 1) * TTW)
                    vector.tensor_mul(tfa, xf(c), t["PSA"][:, 0:512])
                    vector.tensor_mul(tfb, xf(c + 1), t["PSA"][:, 0:512])
                    vector.drain()
                    vector.tensor_add(LNB[:, csa], tfa, t["PSB"][:, 0:512])
                    vector.tensor_add(LNB[:, csb], tfb, t["PSB"][:, 0:512])
                    vector.drain()
                p.inc(vector.drain(), "vE")
                return
            for c in range(DC):
                tf = t["TF"][:, :]
                if False:
                    pass
                else:
                    cs = slice(c * TC + tt * TTW, c * TC + (tt + 1) * TTW)
                    tf2 = t["TF2"][:, 0:TTW]
                    vector.tensor_mul(tf, rsl(c)[:, tok], t["PSA"][:, 0:512])
                    vector.drain()
                    vector.tensor_add(tf2, tf, t["PSB"][:, 0:512])
                    vector.drain()
                    vector.tensor_scalar(tf, tf2, t["LNWC"][:, c:c + 1],
                                         t["LNBC"][:, c:c + 1], OP.mult, OP.add)
                    vector.drain()
                    vector.tensor_add(t["X"][:, cs], t["X"][:, cs], tf)
                    vector.drain()
            p.inc(vector.drain(), "vE")

        block.vector(b_nm)

    ctx[f"{tag}_done_v"] = p.n["vE"]


def ph_moe_gates(p, io, ctx, tag):
    t, block = p.t, p.block
    X, LNB = t["X"], t["LNB"]
    PSA, PSB, PSV, PSW = t["PSA"], t["PSB"], t["PSV"], t["PSW"]

    t1_v, v1_v, s1_v, v2_v, t2_v, s2_v = {}, {}, {}, {}, {}, {}
    for tt in range(NTT):
        for j in range(4):
            jt = tt * 4 + j
            par = jt % 2
            lg = PSW[:, 512 + par * 8:520 + par * 8]
            B = 16 + par * 64

            def b_t1(tensor, jt=jt, lg=lg):
                if jt == 0:
                    p.bar(tensor, ["vE"])
                if jt >= 2:
                    p.w(tensor, "vE", v2_v[jt - 2])
                for c in range(DC):
                    tensor.matmul(lg, LNB[:, c * TC + jt * 128: c * TC + (jt + 1) * 128],
                                  t["GWB"][:, c * 8:(c + 1) * 8], start=(c == 0), stop=False)
                tensor.matmul(lg, t["ONER"][:], t["GBB"][:], start=False, stop=True)
                p.inc(tensor.drain(), "tE")

            block.tensor(b_t1)
            t1_v[jt] = p.n["tE"]

            def b_v1(vector, jt=jt, lg=lg, B=B):
                p.w(vector, "tE", t1_v[jt])
                vector.reduce_max(t["TF"][:, B + 0:B + 1], lg, axis=mybir.AxisListType.X)
                vector.drain()
                vector.tensor_scalar(t["TF"][:, B + 8:B + 16], lg, t["TF"][:, B + 0:B + 1],
                                     None, OP.is_equal)
                vector.drain()
                vector.scalar_tensor_tensor(t["TF"][:, B + 16:B + 24], t["TF"][:, B + 8:B + 16],
                                            -1e9, lg, OP.mult, OP.add)
                vector.drain()
                vector.reduce_max(t["TF"][:, B + 1:B + 2], t["TF"][:, B + 16:B + 24],
                                  axis=mybir.AxisListType.X)
                vector.tensor_scalar_mul(t["TF"][:, B + 3:B + 4], t["TF"][:, B + 0:B + 1], -1.0)
                p.inc(vector.drain(), "vE")

            block.vector(b_v1)
            v1_v[jt] = p.n["vE"]

            def b_s1(scalar, jt=jt, lg=lg, B=B):
                p.w(scalar, "vE", v1_v[jt])
                scalar.activation(t["TF"][:, B + 32:B + 40], lg, AF.Exp,
                                  bias=t["TF"][:, B + 3:B + 4], scale=1.0,
                                  accum_out=t["TF"][:, B + 4:B + 5])
                p.inc(scalar.drain(), "sE")

            block.scalar(b_s1)
            s1_v[jt] = p.n["sE"]

            def b_v2(vector, jt=jt, lg=lg, B=B, par=par):
                p.w(vector, "sE", s1_v[jt])
                if jt >= 2:
                    p.w(vector, "tE", t2_v[jt - 2])
                w = t["WTM"][:, par * 8:(par + 1) * 8]
                vector.tensor_scalar(t["TF"][:, B + 24:B + 32], lg, t["TF"][:, B + 1:B + 2],
                                     None, OP.is_ge)
                vector.reciprocal(t["TF"][:, B + 5:B + 6], t["TF"][:, B + 4:B + 5])
                vector.drain()
                vector.tensor_scalar_mul(t["TF"][:, B + 6:B + 7], t["TF"][:, B + 5:B + 6], 0.5)
                vector.drain()
                vector.tensor_mul(t["TF"][:, B + 24:B + 32], t["TF"][:, B + 32:B + 40],
                                  t["TF"][:, B + 24:B + 32])
                vector.drain()
                vector.tensor_scalar(w, t["TF"][:, B + 24:B + 32], t["TF"][:, B + 6:B + 7],
                                     None, OP.mult)
                p.inc(vector.drain(), "vE")

            block.vector(b_v2)
            v2_v[jt] = p.n["vE"]

            def b_t2(tensor, tt=tt, j=j, jt=jt, par=par):
                p.w(tensor, "vE", v2_v[jt])
                if tt >= 1:
                    p.w(tensor, "sE", s2_v[tt - 1])
                tensor.matmul(PSV[0:8, j * 128:(j + 1) * 128],
                              t["WTM"][:, par * 8:(par + 1) * 8],
                              t["EYEB"][:], start=True, stop=True)
                p.inc(tensor.drain(), "tE")

            block.tensor(b_t2)
            t2_v[jt] = p.n["tE"]

        def b_s2(scalar, tt=tt):
            p.w(scalar, "tE", t2_v[tt * 4 + 3])
            scalar.copy(t["WROW"][:, tt * TTW:(tt + 1) * TTW], PSV[0:8, 0:512])
            p.inc(scalar.drain(), "sE")

        block.scalar(b_s2)
        s2_v[tt] = p.n["sE"]

        def b_v3(vector, tt=tt):
            p.w(vector, "sE", s2_v[tt])
            vector.tensor_copy(t["WMSK"][:, tt * TTW:(tt + 1) * TTW],
                               t["WROW"][:, tt * TTW:(tt + 1) * TTW])
            p.inc(vector.drain(), "vE")

        block.vector(b_v3)

    ctx[f"{tag}_gates_v"] = p.n["vE"]

    if _MOE_GATES_ONLY:
        def b_fence0(vector):
            p.bar(vector, ["vE", "sE", "tE"])
            p.inc(vector.drain(), "vE")
        block.vector(b_fence0)
        ctx[f"{tag}_done_v"] = p.n["vE"]
        return

    wm_d0 = p.n["dW"]

    def b_wmw(sync):
        p.bar(sync, ["vE"])
        p.inc(sync.dma_start(out=io["wmskd"][:, :], in_=t["WMSK"][0:8, :]), "dW")

    block.sync(b_wmw)
    ctx[f"{tag}_wm_dW"] = wm_d0 + 16


def ph_moe(p, io, ctx, tag):
    t, block = p.t, p.block
    X, LNB = t["X"], t["LNB"]
    PSA, PSB, PSV, PSW = t["PSA"], t["PSB"], t["PSV"], t["PSW"]

    if f"{tag}_gates_v" not in ctx:
        ph_moe_gates(p, io, ctx, tag)
        if _MOE_GATES_ONLY:
            return
    wm_d0 = ctx[f"{tag}_wm_dW"] - 16
    gB0 = p.n["gB"]

    def b_wmd(gp):
        p.w(gp, "dW", wm_d0 + 16)
        for tt in range(NTT):
            for e in range(E):
                idx = tt * E + e
                if tt > 0:
                    p.w(gp, "xe", ctx[f"{tag}_xe0"] + (tt - 1) * E + e + 1)
                srcap = bass.AP(io["wmskd"], e * TC + tt * TTW, [[0, 128], [1, TTW]])
                p.inc(gp.dma_start(out=t["WBC"][:, e * TTW:(e + 1) * TTW], in_=srcap),
                      "dG")
                p.w(gp, "dG", p.n["dG"])
                gp.sem_inc(p.s["gB"], 1)
                p.n["gB"] += 1

    ctx[f"{tag}_xe0"] = p.n["xe"]
    block.gpsimd(b_wmd)

    xe0 = ctx[f"{tag}_xe0"]
    wg0 = p.n["wg"]
    d0 = p.n["dW"]


    wlb = [p.n["wl0"], p.n["wl1"]]

    def b_wdma2(sync):
        for tt in range(NTT):
            for e in range(E):
                for half in range(2):
                    gidx = tt * 16 + e * 2 + half
                    par = gidx % 2
                    if gidx >= 2:
                        p.w(sync, "wg2", ctx[f"{tag}_wg2_0"] + gidx - 1)
                    gsrc = half * 8 + e
                    for j in (2 * gsrc, 2 * gsrc + 1):
                        p.w(sync, ctx["moe_scr_sem"][j], ctx["moe_scr_wb"][j])
                    p.inc(sync.dma_start(out=t["WG"][:, par * 4096:(par + 1) * 4096],
                                         in_=io["wscr_moe"][:, gsrc * 4096:(gsrc + 1) * 4096]),
                          f"wl{par}")

    ctx[f"{tag}_wg2_0"] = p.n["wg2"]
    block.sync(b_wdma2)

    def bank(half, oc):
        ps = (PSA, PSB, PSV, PSW)[half * 2 + oc // 2]
        return ps[:, (oc % 2) * 512:(oc % 2 + 1) * 512]

    def b_mm(tensor):
        for tt in range(NTT):
            tok = slice(tt * TTW, (tt + 1) * TTW)
            for e in range(E):
                epar = (tt * E + e) % 2
                p.w(tensor, "xe", xe0 + tt * E + e + 1)
                for half in range(2):
                    gidx = tt * 16 + e * 2 + half
                    par = gidx % 2
                    p.w(tensor, f"wl{par}", wlb[par] + 16 * (gidx // 2 + 1))
                    if e == 0 and half == 0 and tt > 0:
                        p.w(tensor, "ev", ctx[f"{tag}_ev0"] + 2 * tt)
                    for oc in range(4):
                        for kc in range(DC):
                            tensor.matmul(
                                bank(half, oc),
                                t["WG"][:, par * 4096 + (oc * DC + kc) * 128:
                                        par * 4096 + (oc * DC + kc) * 128 + 128],
                                t["XE"][:, (epar * DC + kc) * TTW:(epar * DC + kc + 1) * TTW],
                                start=(e == 0 and kc == 0), stop=False)
                    p.inc(tensor.drain(), "wg2")
                p.inc(tensor.drain(), "wg")
            for half in range(2):
                for oc in range(4):
                    o = half * 4 + oc
                    tensor.matmul(bank(half, oc), t["EBB"][0:8, o * 128:(o + 1) * 128],
                                  t["WMSK"][0:8, tok], start=False, stop=True)
                p.inc(tensor.drain(), "tE")
                ctx.setdefault(f"{tag}_half_t", []).append(p.n["tE"])

    ctx[f"{tag}_ev0"] = p.n["ev"]
    block.tensor(b_mm)
    half_ts = ctx[f"{tag}_half_t"]

    def ev_pass(vector, tt):
        for half in range(2):
            p.w(vector, "tE", half_ts[tt * 2 + half])
            for oc in range(4):
                o = half * 4 + oc
                cs = slice(o * TC + tt * TTW, o * TC + (tt + 1) * TTW)
                vector.tensor_add(X[:, cs], X[:, cs], bank(half, oc))
            p.inc(vector.drain(), "ev")

    def xe_one(vector, tt, e):
        idx = tt * E + e
        par = idx % 2
        p.w(vector, "gB", gB0 + idx + 1)
        if idx >= 2:
            p.w(vector, "wg", wg0 + idx - 1)
        for c in range(DC):
            vector.tensor_mul(
                t["XE"][:, (par * DC + c) * TTW:(par * DC + c + 1) * TTW],
                LNB[:, c * TC + tt * TTW: c * TC + (tt + 1) * TTW],
                t["WBC"][:, e * TTW:(e + 1) * TTW])
        p.inc(vector.drain(), "xe")

    def b_vec(vector):
        for tt in range(NTT):
            for e in range(E):
                xe_one(vector, tt, e)
                if tt >= 1 and e == 1:
                    ev_pass(vector, tt - 1)
        ev_pass(vector, NTT - 1)

    block.vector(b_vec)

    ctx[f"{tag}_done_ev"] = p.n["ev"]

    def b_fence(vector):
        p.w(vector, "ev", ctx[f"{tag}_done_ev"])
        p.inc(vector.drain(), "vE")

    block.vector(b_fence)
    ctx[f"{tag}_done_v"] = p.n["vE"]


def ph_qkv(p, io, ctx):
    t, block = p.t, p.block
    LNB = t["LNB"]
    PSA, PSB, PSW = t["PSA"], t["PSB"], t["PSW"]

    ln_done = ctx["ln2_done_v"]

    def b_aux(sync):
        p.bar(sync, ["tE", "vE", "ev", "sE"])
        p.inc(sync.dma_start(out=t["RSC"][:], in_=io["rsc_s"]), "dW")
        p.inc(sync.dma_start(out=t["RSC2"][:], in_=io["rcc_s"]), "dW")


    p.block.sync(b_aux)
    aux_dW = p.n["dW"]

    def b_auxc(scalar):
        p.w(scalar, "dW", aux_dW)
        scalar.copy(t["TF"][0:1, 0:1], t["RSC"][0:1, 0:1])
        p.inc(scalar.drain(), "sE")

    p.block.scalar(b_auxc)
    aux_s = p.n["sE"]

    s0, wg0, cast_piece, cast_sE = stream_cast(
        p, io, ctx, "wqkv_s", 12, "qkv", wait_dW=ctx["prep_done_dW"],
        extra_waits=tuple(ctx["hol_scr_hw"].values()), defer_casts=True)
    cast_piece(0)
    cast_piece(1)

    tab_s = p.n["sE"]
    ev0 = p.n["ev"]
    ctx["qkv_qs0"] = [p.n[f"qs{i}"] for i in range(4)]
    VS_SEMS = ("vs0", "vs1", "gs0", "gs1")
    ctx["qkv_vs0"] = [p.n[nm] for nm in VS_SEMS]
    ctx["qkv_vs_sems"] = VS_SEMS

    u = 0
    for wi, which in enumerate(("q", "k")):
        for pair in range(NPAIR):
            piece = wi * 4 + pair // 2
            par = piece % 2
            wcol0 = par * 2048 + (pair % 2) * 1024
            if pair % 2 == 0 and 2 <= piece + 1 < 12:
                cast_piece(piece + 1)
            for qb in range(4):
                bank = PSA[:, 0:512] if u % 2 == 0 else PSB[:, 0:512]

                def b_mm(tensor, piece=piece, wcol0=wcol0, bank=bank, qb=qb, u=u,
                         pair=pair):
                    p.w(tensor, "wb", cast_sE[piece])
                    p.w(tensor, "sE", tab_s)
                    p.w(tensor, "vE", ln_done)
                    if u >= 2:
                        p.w(tensor, "ev", ev0 + 2 * u - 3)
                    for kc in range(DC):
                        tensor.matmul(bank,
                                      t["WSB"][:, wcol0 + kc * 128: wcol0 + kc * 128 + 128],
                                      LNB[:, kc * TC + qb * TTW: kc * TC + (qb + 1) * TTW],
                                      start=(kc == 0), stop=(kc == DC - 1))
                    p.inc(tensor.drain(), "tE")
                    if qb == 3 and (pair % 2) == 1:
                        p.inc(tensor.drain(), "wg")

                block.tensor(b_mm)
                mm_t = p.n["tE"]

                def b_rope(vector, u=u, qb=qb, which=which, pair=pair,
                           bank=bank, mm_t=mm_t):
                    p.w(vector, "tE", mm_t)
                    if u >= 4:
                        uu4 = u - 4
                        p.w(vector, f"qs{uu4 % 4}", ctx["qkv_qs0"][uu4 % 4] + 16 * (uu4 // 4 + 1))
                    raw = t["QRAW"][:, (u % 2) * TTW:((u % 2) + 1) * TTW]
                    vector.tensor_copy(raw, bank)
                    p.inc(vector.drain(), "ev")
                    t1 = t["TF"][:, 0:TTW]
                    t2 = t["TF2"][:, 0:TTW]
                    vector.tensor_mul(t1, raw, t["RSC"][:, qb * TTW:(qb + 1) * TTW])
                    vector.drain()
                    perm = [(i + 1 if i % 2 == 0 else i - 1) for i in range(32)]
                    vector.stream_shuffle(t2, t1, perm)
                    vector.drain()
                    vector.tensor_mul(t1, raw, t["RSC2"][:, qb * TTW:(qb + 1) * TTW])
                    vector.drain()
                    slot = (u % 4) * TTW
                    vector.scalar_tensor_tensor(t["BNC"][:, slot:slot + TTW], t2,
                                                t["SGN"][:], t1, OP.mult, OP.add)
                    p.inc(vector.drain(), "ev")

                block.vector(b_rope)
                rp_v = p.n["ev"]

                def b_st(gp, which=which, pair=pair, qb=qb, rp_v=rp_v, u=u):
                    p.w(gp, "ev", rp_v)
                    slot = (u % 4) * TTW
                    if which == "q":
                        dst = io["kqstage"][:, QOFF + pair * TC + qb * TTW:
                                            QOFF + pair * TC + (qb + 1) * TTW]
                        src = t["BNC"][:, slot:slot + TTW]
                    else:
                        dst = io["kqstage"][:, 0:NPAIR * TC].rearrange(
                            "p (t w) -> p t w", w=1024
                        )[:, qb * 4:(qb + 1) * 4, pair * 128:(pair + 1) * 128]
                        src = t["BNC"][:, slot:slot + TTW].rearrange(
                            "p (t w) -> p t w", w=128)
                    p.inc(gp.dma_start(out=dst, in_=src),
                          f"qs{u % 4}")

                block.gpsimd(b_st)
                u += 1

    def b_cckq(gp):
        for i in range(4):
            p.w(gp, f"qs{i}", p.n[f"qs{i}"])
        p.inc(gp.collective_compute(
            "AllGather", OP.bypass,
            replica_groups=[[0, 1], [2, 3], [4, 5], [6, 7]],
            ins=[io["kqstage"].ap().opt()],
            outs=[io["ckqo"].ap().opt()],
        ), "cc")

    block.gpsimd(b_cckq)
    ctx["cc_kq"] = p.n["cc"]

    for grp in range(2):
        if grp == 0:
            cast_piece(9)
        else:
            cast_piece(10)
            cast_piece(11)
        for jt in range(KT):
            uu = grp * KT + jt
            bank = PSA[:, 0:512] if uu % 2 == 0 else PSB[:, 0:512]

            def b_vm(tensor, grp=grp, jt=jt, bank=bank, uu=uu):
                p.w(tensor, "wb", cast_sE[8 + grp * 2 + 1])
                p.w(tensor, "ev", ev0 + 128 + max(0, uu - 1))
                for kc in range(DC):
                    pc = 8 + grp * 2 + (kc // 4)
                    wcol = (pc % 2) * 2048 + (kc % 4) * 512
                    tensor.matmul(bank,
                                  LNB[:, kc * TC + jt * 128: kc * TC + (jt + 1) * 128],
                                  t["WSB"][:, wcol: wcol + 512],
                                  start=(kc == 0), stop=(kc == DC - 1))
                p.inc(tensor.drain(), "tE")
                if jt == KT - 1:
                    p.inc(tensor.drain(), "wg")
                    p.inc(tensor.drain(), "wg")

            block.tensor(b_vm)
            vm_t = p.n["tE"]

            def b_vev(vector, uu=uu, bank=bank, vm_t=vm_t):
                p.w(vector, "tE", vm_t)
                if uu >= 4:
                    sl = uu % 4
                    p.w(vector, ctx["qkv_vs_sems"][sl],
                        ctx["qkv_vs0"][sl] + 16 * (uu // 4))
                dstp = uu % 4
                dst = t["VBO"][:, dstp * 520: dstp * 520 + 520].rearrange("p (h w) -> p h w", w=65)
                src = bank.rearrange("p (h w) -> p h w", w=64)
                vector.tensor_copy(dst[:, :, 0:64], src)
                vector.memset(dst[:, :, 64:65], 1.0)
                p.inc(vector.drain(), "ev")

            block.vector(b_vev)
            vev_v = p.n["ev"]

            def b_vst(gp, grp=grp, jt=jt, uu=uu, vev_v=vev_v):
                p.w(gp, "ev", vev_v)
                dstp = uu % 4
                dst = io[f"vstage{grp}"][:, jt * 520:(jt + 1) * 520]
                p.inc(gp.dma_start(out=dst, in_=t["VBO"][:, dstp * 520:(dstp + 1) * 520]),
                      ctx["qkv_vs_sems"][uu % 4])

            block.gpsimd(b_vst)

        def b_ccv(gp, grp=grp, vals=[(nm, p.n[nm]) for nm in ctx["qkv_vs_sems"]]):
            for nm, v in vals:
                p.w(gp, nm, v)
            p.inc(gp.collective_compute(
                "AllGather", OP.bypass,
                replica_groups=[[0, 1], [2, 3], [4, 5], [6, 7]],
                ins=[io[f"vstage{grp}"].ap().opt()],
                outs=[io[f"cvo{grp}"].ap().opt()],
            ), "cc")

        block.gpsimd(b_ccv)
        ctx[f"cc_v{grp}"] = p.n["cc"]

    ctx["qkv_vs_end"] = [p.n[nm] for nm in ctx["qkv_vs_sems"]]


def ph_attn(p, io, ctx):
    t, block = p.t, p.block
    PSA, PSB, PSV, PSW = t["PSA"], t["PSB"], t["PSV"], t["PSW"]
    LNB = t["LNB"]

    ev0 = p.n["ev"]
    ex0 = p.n["ex"]
    unit_no = 0
    ex_cum = 0
    rg = 0
    dg = 0
    pv_rel = {}
    pl_rel = {}

    ctx["attn_ob0"] = [p.n["ob0"], p.n["ob1"]]
    cl_base = [p.n["cl0"], p.n["cl1"]]
    gi_base = [p.n["gi0"], p.n["gi1"]]

    def b_rect(gp):
        with gp.register("rtb") as rn:
            gp.reg_load(rn, t["RTBI"][0:1, 0:1])
            tb = gp.snap(rn, min_val=0, max_val=8)
            p.w(gp, "cc", ctx["cc_kq"])
            kq0 = io["ckqo"].ap()[0:1].squeeze(0)
            kv = kq0[:, 0:NPAIR * TC].rearrange(
                "p (t w) -> t p w", t=KT, w=1024)
            dk = io["rectkv"][:, 0:RVOFF].rearrange(
                "p (t w) -> t p w", t=8, w=1024)
            p.inc(gp.dma_start(out=dk, in_=kv[bass.ds(tb, 8)]), "dG")
            for grp in range(2):
                p.w(gp, "cc", ctx[f"cc_v{grp}"])
                if grp == 1:
                    p.w(gp, "dG", p.n["dG"])
                v0 = io[f"cvo{grp}"].ap()[0:1].squeeze(0)
                vv = v0.rearrange("p (t hw) -> t p hw", t=KT, hw=8 * 65)
                dv = io["rectkv"][:, RVOFF + grp * 4160:
                                  RVOFF + (grp + 1) * 4160].rearrange(
                    "p (t hw) -> t p hw", t=8, hw=8 * 65)
                p.inc(gp.dma_start(out=dv, in_=vv[bass.ds(tb, 8)]), "dG")

    block.gpsimd(b_rect)
    rect_dG = p.n["dG"]
    rect_dG_v0 = rect_dG - 16

    for sg in range(4):
        pairs = (2 * sg, 2 * sg + 1)

        def b_ld(sync, sg=sg, pairs=pairs):
            p.w(sync, "cc", ctx["cc_kq"])
            for i, nm in enumerate(ctx["qkv_vs_sems"]):
                p.w(sync, nm, ctx["qkv_vs_end"][i])
            p.w(sync, "dG", rect_dG_v0 if sg < 2 else rect_dG)
            if sg > 0:
                p.w(sync, "tE", ctx["attn_sg_t"][sg - 1])
            else:
                p.inc(sync.dma_start(out=t["DMSK"][:], in_=io["dmask_s"]), "dW")
            cc1r1 = io["ckqo"].ap()[1:2].squeeze(0)
            for i, pair in enumerate(pairs):
                p.inc(sync.dma_start(out=t["QG"][:, i * TC:(i + 1) * TC],
                                     in_=io["kqstage"][:, QOFF + pair * TC:
                                                       QOFF + (pair + 1) * TC]), "dW")
                ksrc = io["kqstage"][:, 0:NPAIR * TC].rearrange(
                    "p (t w) -> p t w", w=1024)[:, :, pair * 128:(pair + 1) * 128]
                kdst = t["KG"][:, i * TC:(i + 1) * TC].rearrange(
                    "p (t w) -> p t w", w=128)
                p.inc(sync.dma_start(out=kdst, in_=ksrc), "dW")
                p.inc(sync.dma_start(out=t["QR"][:, i * TC:(i + 1) * TC],
                                     in_=cc1r1[:, QOFF + pair * TC:
                                               QOFF + (pair + 1) * TC]), "dW")
                krsrc = io["rectkv"][:, 0:RVOFF].rearrange(
                    "p (t w) -> p t w", w=1024)[:, :, pair * 128:(pair + 1) * 128]
                krdst = t["KR"][:, i * 1024:(i + 1) * 1024].rearrange(
                    "p (t w) -> p t w", w=128)
                p.inc(sync.dma_start(out=krdst, in_=krsrc), "dW")
            g = sg // 2
            vfull = io[f"vstage{g}"][:, 0:KT * 8 * 65].rearrange(
                "p (t w) -> p t w", w=8 * 65)
            rvfull = io["rectkv"][:, RVOFF + g * 4160:RVOFF + (g + 1) * 4160].rearrange(
                "p (t w) -> p t w", w=8 * 65)
            for h4 in range(4):
                h = sg * 4 + h4
                src = vfull[:, :, (h % 8) * 65:((h % 8) + 1) * 65]
                dst = t["VG"][:, h4 * KT * 65:(h4 + 1) * KT * 65].rearrange(
                    "p (t w) -> p t w", w=65)
                p.inc(sync.dma_start(out=dst, in_=src), "dW")
                src2 = rvfull[:, :, (h % 8) * 65:((h % 8) + 1) * 65]
                dst2 = t["VR"][:, h4 * 8 * 65:(h4 + 1) * 8 * 65].rearrange(
                    "p (t w) -> p t w", w=65)
                p.inc(sync.dma_start(out=dst2, in_=src2), "dW")

        block.sync(b_ld)
        ld_dW = p.n["dW"]

        for pi in range(2):
            for qb in range(4):
                tiles = [("r", kt2, None) for kt2 in range(8)]
                ng = len(tiles)
                u = unit_no
                tE_u0 = p.n["tE"]
                ex_u0 = ex0 + ex_cum
                pv = PSV if u % 2 == 0 else PSW

                def b_tq(tensor, pi=pi, qb=qb, tiles=tiles, ng=ng, u=u,
                         tE_u0=tE_u0, ex_u0=ex_u0, pv=pv, ld_dW=ld_dW,
                         gt0=ex_cum):
                    p.w(tensor, "dW", ld_dW)
                    if u >= 2:
                        p.w(tensor, "ev", pv_rel[u - 2])
                    for gi, (kind, kt2, moff) in enumerate(tiles):
                        gt = gt0 + gi
                        if gt >= 2:
                            p.w(tensor, "ex", ex0 + gt - 1)
                        kcol = pi * 1024 + kt2 * 128
                        sbk = PSA if gi % 2 == 0 else PSB
                        tensor.matmul(sbk[:, 0:512], t["KR"][0:64, kcol:kcol + 128],
                                      t["QR"][0:64, pi * TC + qb * TTW:
                                              pi * TC + (qb + 1) * TTW],
                                      start=True, stop=True, tile_position=(0, 0))
                        tensor.matmul(sbk[:, 512:1024], t["KR"][64:128, kcol:kcol + 128],
                                      t["QR"][64:128, pi * TC + qb * TTW:
                                              pi * TC + (qb + 1) * TTW],
                                      start=True, stop=True, tile_position=(64, 0))
                        p.inc(tensor.drain(), "tE")
                        if gi >= 1:
                            _av(p, t, tensor, pi, tiles, gi - 1, ng, u, ex_u0, 0, pv, gt0)
                    _av(p, t, tensor, pi, tiles, ng - 1, ng, u, ex_u0, 0, pv, gt0)
                    p.inc(tensor.drain(), "tE")

                block.tensor(b_tq)

                def b_ex(scalar, tiles=tiles, ng=ng, tE_u0=tE_u0, gt0=ex_cum):
                    for gi in range(ng):
                        p.w(scalar, "tE", tE_u0 + gi + 1)
                        par = (gt0 + gi) % 4
                        sbk = PSA if gi % 2 == 0 else PSB
                        scalar.activation(t["PT"][:, par * 1024:(par + 1) * 1024],
                                          sbk[:, 0:1024], AF.Exp, bias=t["ZEROC"][:],
                                          scale=0.125)
                        p.inc(scalar.drain(), "ex")

                block.scalar(b_ex)
                ex_cum += ng

                def b_rp_ev(vector, rg=rg, u=u, tE_u0=tE_u0, ng=ng, pv=pv):
                    p.w(vector, "tE", tE_u0 + ng + 1)
                    if rg >= 2:
                        r2 = rg - 2
                        p.w(vector, f"cl{r2 % 2}",
                            cl_base[r2 % 2] + 16 * (r2 // 2 + 1))
                    slot = (rg % 2) * 1024
                    vector.tensor_copy(t["RP"][:, slot:slot + 1024],
                                       pv[:, 0:1024])
                    p.inc(vector.drain(), "ev")

                block.vector(b_rp_ev)
                pv_rel[u] = p.n["ev"]
                rpev = p.n["ev"]

                def b_rp_st(sync, sg=sg, rg=rg, pi=pi, qb=qb, rpev=rpev):
                    p.w(sync, "ev", rpev)
                    ru = pi * 4 + qb
                    slot = (rg % 2) * 1024
                    p.inc(sync.dma_start(
                        out=io[f"rectst{sg}"][:, ru * 1024:(ru + 1) * 1024],
                        in_=t["RP"][:, slot:slot + 1024]), f"cl{rg % 2}")

                block.sync(b_rp_st)
                rg += 1
                unit_no += 1

        def b_cc3(gp, sg=sg, q2=p.n["cl0"], q3=p.n["cl1"]):
            p.w(gp, "cl0", q2)
            p.w(gp, "cl1", q3)
            p.inc(gp.collective_compute(
                "AllGather", OP.bypass,
                replica_groups=[[0, 1], [2, 3], [4, 5], [6, 7]],
                ins=[io[f"rectst{sg}"].ap().opt()],
                outs=[io[f"cc3o{sg}"].ap().opt()],
            ), "cc")

        block.gpsimd(b_cc3)
        cc3_done = p.n["cc"]

        pend_zbnm = [None]
        for pi in range(2):
            for qb in range(4):
                tiles = [("d", tt2, None) for tt2 in range(qb * 4)]
                tiles += [("d", qb * 4 + i4, i4) for i4 in range(4)]
                ng = len(tiles)
                u = unit_no
                tE_u0 = p.n["tE"]
                ex_u0 = ex0 + ex_cum
                mk_u0 = p.n["mk"]
                pv = PSV if u % 2 == 0 else PSW

                def b_tq(tensor, pi=pi, qb=qb, tiles=tiles, ng=ng, u=u,
                         tE_u0=tE_u0, ex_u0=ex_u0, pv=pv, ld_dW=ld_dW,
                         gt0=ex_cum):
                    p.w(tensor, "dW", ld_dW)
                    if u >= 2:
                        p.w(tensor, "ev", pv_rel[u - 2])
                    for gi, (kind, kt2, moff) in enumerate(tiles):
                        gt = gt0 + gi
                        if gt >= 2:
                            p.w(tensor, "ex", ex0 + gt - 1)
                        kcol = pi * TC + kt2 * 128
                        sbk = PSA if gi % 2 == 0 else PSB
                        tensor.matmul(sbk[:, 0:512], t["KG"][0:64, kcol:kcol + 128],
                                      t["QG"][0:64, pi * TC + qb * TTW:
                                              pi * TC + (qb + 1) * TTW],
                                      start=True, stop=True, tile_position=(0, 0))
                        tensor.matmul(sbk[:, 512:1024], t["KG"][64:128, kcol:kcol + 128],
                                      t["QG"][64:128, pi * TC + qb * TTW:
                                              pi * TC + (qb + 1) * TTW],
                                      start=True, stop=True, tile_position=(64, 0))
                        p.inc(tensor.drain(), "tE")
                        if gi >= 1:
                            _av(p, t, tensor, pi, tiles, gi - 1, ng, u, ex_u0, mk_u0, pv, gt0)
                    _av(p, t, tensor, pi, tiles, ng - 1, ng, u, ex_u0, mk_u0, pv, gt0)
                    p.inc(tensor.drain(), "tE")

                block.tensor(b_tq)

                def b_ex(scalar, tiles=tiles, ng=ng, tE_u0=tE_u0, mk_u0=mk_u0,
                         gt0=ex_cum):
                    for gi, (kind, kt2, moff) in enumerate(tiles):
                        p.w(scalar, "tE", tE_u0 + gi + 1)
                        if gi >= 2 and tiles[gi - 2][2] is not None:
                            j = tiles[gi - 2][2]
                            p.w(scalar, "mk", mk_u0 + j + 1)
                        par = (gt0 + gi) % 4
                        sbk = PSA if gi % 2 == 0 else PSB
                        scalar.activation(t["PT"][:, par * 1024:(par + 1) * 1024],
                                          sbk[:, 0:1024], AF.Exp, bias=t["ZEROC"][:],
                                          scale=0.125)
                        p.inc(scalar.drain(), "ex")

                block.scalar(b_ex)
                ex_cum += ng

                def b_mk(vector, tiles=tiles, ex_u0=ex_u0, gt0=ex_cum):
                    for gi, (kind, kt2, moff) in enumerate(tiles):
                        if moff is None:
                            continue
                        p.w(vector, "ex", ex_u0 + gi + 1)
                        par = (gt0 + gi) % 4
                        for hh in range(2):
                            sl = slice(par * 1024 + hh * 512, par * 1024 + (hh + 1) * 512)
                            vector.tensor_mul(t["PT"][:, sl], t["PT"][:, sl],
                                              t["DMSK"][:, moff * 512:(moff + 1) * 512])
                        p.inc(vector.drain(), "mk")

                block.vector(b_mk)

                if pend_zbnm[0] is not None:
                    pend_zbnm[0]()
                    pend_zbnm[0] = None

                def b_pl(sync, sg=sg, dg=dg, pi=pi, qb=qb, cc3_done=cc3_done):
                    p.w(sync, "cc", cc3_done)
                    if dg >= 2:
                        p.w(sync, "ev", pl_rel[dg - 2])
                    du = pi * 4 + qb
                    slot = t["PL0"] if dg % 2 == 0 else t["PL1"]
                    for r in range(2):
                        src = io[f"cc3o{sg}"].ap()[r:r + 1].squeeze(0)
                        p.inc(sync.dma_start(
                            out=slot[0:65, r * 1024:(r + 1) * 1024],
                            in_=src[0:65, du * 1024:(du + 1) * 1024]), f"gi{dg % 2}")

                block.sync(b_pl)

                def b_padd(vector, dg=dg, tE_u0=tE_u0, ng=ng, pv=pv):
                    p.w(vector, "tE", tE_u0 + ng + 1)
                    p.w(vector, f"gi{dg % 2}", gi_base[dg % 2] + 32 * (dg // 2 + 1))
                    slot = t["PL0"] if dg % 2 == 0 else t["PL1"]
                    vector.tensor_add(slot[0:65, 0:1024], slot[0:65, 0:1024],
                                      slot[0:65, 1024:2048])
                    vector.drain()
                    vector.scalar_tensor_tensor(pv[0:65, 0:1024], slot[0:65, 0:1024],
                                                t["CMC"][0:65, :], pv[0:65, 0:1024],
                                                OP.mult, OP.add)
                    vector.drain()

                block.vector(b_padd)

                def b_ev(vector, pi=pi, qb=qb, sg=sg, ng=ng, tE_u0=tE_u0, pv=pv, dg=dg):
                    p.w(vector, "tE", tE_u0 + ng + 1)
                    if dg >= 2:
                        d2 = dg - 2
                        p.w(vector, f"ob{d2 % 2}", ctx["attn_ob0"][d2 % 2] + 16 * (d2 // 2 + 1))
                    pair = sg * 2 + pi
                    dst = LNB[:, pair * TC + qb * TTW: pair * TC + (qb + 1) * TTW]
                    vector.tensor_copy(dst[0:64, :], pv[0:64, 0:512])
                    vector.tensor_copy(t["OB"][0:64, (dg % 2) * 512:(dg % 2 + 1) * 512],
                                       pv[0:64, 512:1024])
                    vector.reciprocal(t["ZR"][64:65, 0:1024], pv[64:65, 0:1024])
                    p.inc(vector.drain(), "ev")

                block.vector(b_ev)
                ev_u = p.n["ev"]
                pl_rel[dg] = ev_u

                def b_obd(sync, pi=pi, qb=qb, sg=sg, dg=dg, ev_u=ev_u):
                    p.w(sync, "ev", ev_u)
                    pair = sg * 2 + pi
                    dst = LNB[:, pair * TC + qb * TTW: pair * TC + (qb + 1) * TTW]
                    p.inc(sync.dma_start(out=dst[64:128, :],
                                         in_=t["OB"][0:64, (dg % 2) * 512:(dg % 2 + 1) * 512]),
                          f"ob{dg % 2}")

                block.sync(b_obd)
                obd_u = (f"ob{dg % 2}", ctx["attn_ob0"][dg % 2] + 16 * (dg // 2 + 1))

                def emit_zbnm(pi=pi, qb=qb, sg=sg, pv=pv, ev_u=ev_u, obd_u=obd_u,
                              u=u):
                    def b_zb(tensor, pv=pv, ev_u=ev_u):
                        p.w(tensor, "ev", ev_u)
                        for hh in range(2):
                            tensor.matmul(pv[hh * 64:(hh + 1) * 64, 0:512],
                                          t["ONESF"][64:65, hh * 64:(hh + 1) * 64],
                                          t["ZR"][64:65, hh * 512:(hh + 1) * 512],
                                          start=True, stop=True,
                                          tile_position=(64, hh * 64))
                        p.inc(tensor.drain(), "tE")

                    block.tensor(b_zb)
                    zb_t = p.n["tE"]

                    def b_nm(vector, pi=pi, qb=qb, sg=sg, pv=pv, zb_t=zb_t,
                             obd_u=obd_u):
                        p.w(vector, "tE", zb_t)
                        p.w(vector, obd_u[0], obd_u[1])
                        pair = sg * 2 + pi
                        cs = slice(pair * TC + qb * TTW, pair * TC + (qb + 1) * TTW)
                        vector.tensor_mul(LNB[:, cs], LNB[:, cs], pv[:, 0:512])
                        p.inc(vector.drain(), "ev")

                    block.vector(b_nm)
                    pv_rel[u] = p.n["ev"]

                pend_zbnm[0] = emit_zbnm
                dg += 1
                unit_no += 1

        tight_t = p.n["tE"]
        if pend_zbnm[0] is not None:
            pend_zbnm[0]()
            pend_zbnm[0] = None
        ctx.setdefault("attn_sg_t", {})[sg] = tight_t
        ctx.setdefault("attn_sg_e", {})[sg] = p.n["ev"]

    def b_fence(vector):
        p.w(vector, "ev", p.n["ev"])
        p.inc(vector.drain(), "vE")

    block.vector(b_fence)
    ctx["attn_done_v"] = p.n["vE"]


def _av(p, t, tensor, pi, tiles, gi, ng, u, ex_u0, mk_u0, pv, gt0=0):
    kind, kt2, moff = tiles[gi]
    par = (gt0 + gi) % 4
    p.w(tensor, "ex", ex_u0 + gi + 1)
    if moff is not None:
        p.w(tensor, "mk", mk_u0 + moff + 1)
    Vs = t["VG"] if kind == "d" else t["VR"]
    hstride = (KT * 65) if kind == "d" else (8 * 65)
    for hh in range(2):
        h4 = pi * 2 + hh
        lhsT = Vs[:, h4 * hstride + kt2 * 65: h4 * hstride + (kt2 + 1) * 65]
        tensor.matmul(pv[0:65, hh * 512:(hh + 1) * 512], lhsT,
                      t["PT"][:, par * 1024 + hh * 512: par * 1024 + (hh + 1) * 512],
                      start=(gi == 0), stop=(gi == ng - 1))


def ph_outproj(p, io, ctx):
    t, block = p.t, p.block
    X, LNB = t["X"], t["LNB"]
    PSA, PSB, PSV, PSW = t["PSA"], t["PSB"], t["PSV"], t["PSW"]
    s0, wg0, _, cast_wb_op = stream_cast(p, io, ctx, "wo_s", 4, "op")
    a_done = ctx["attn_done_v"]
    ev0 = p.n["ev"]

    def b_mm(tensor):
        p.w(tensor, "vE", a_done)
        for oc in range(DC):
            piece = oc // 2
            par = piece % 2
            wcol0 = par * 2048 + (oc % 2) * 1024
            for tt in range(NTT):
                u = oc * NTT + tt
                p.w(tensor, "wb", cast_wb_op[piece])
                if u >= 8:
                    p.w(tensor, "ev", ev0 + u - 7)
                bank = (PSA, PSB, PSV, PSW)[(u % 8) // 2][:, (u % 2) * 512:(u % 2) * 512 + 512]
                for af in range(DC):
                    tensor.matmul(bank, t["WSB"][:, wcol0 + af * 128: wcol0 + af * 128 + 128],
                                  LNB[:, af * TC + tt * TTW: af * TC + (tt + 1) * TTW],
                                  start=(af == 0), stop=(af == DC - 1))
                p.inc(tensor.drain(), "tE")
                ctx.setdefault("op_mm_t", []).append(p.n["tE"])
            if oc % 2 == 1:
                p.inc(tensor.drain(), "wg")

    block.tensor(b_mm)

    def b_ev(vector):
        k = 0
        for oc in range(DC):
            for tt in range(NTT):
                p.w(vector, "tE", ctx["op_mm_t"][k]); k += 1
                u = oc * NTT + tt
                bank = (PSA, PSB, PSV, PSW)[(u % 8) // 2][:, (u % 2) * 512:(u % 2) * 512 + 512]
                cs = slice(oc * TC + tt * TTW, oc * TC + (tt + 1) * TTW)
                vector.tensor_add(X[:, cs], X[:, cs], bank)
                p.inc(vector.drain(), "ev")

    block.vector(b_ev)

    def b_fence(vector):
        p.w(vector, "ev", p.n["ev"])
        p.inc(vector.drain(), "vE")

    block.vector(b_fence)
    ctx["op_done_v"] = p.n["vE"]


def ph_hol(p, io, ctx, tag):
    t, block = p.t, p.block
    X, LNB = t["X"], t["LNB"]
    PSA, PSB = t["PSA"], t["PSB"]
    ln_done = ctx[("ln4" if tag == "hol1" else "ln6") + "_done_v"]
    wg0 = p.n["wg"]
    hb0 = [p.n["hb0"], p.n["hb1"]]

    def b_ld(sync):
        p.bar(sync, ["sE", "tE"])
        for j, (nm, val) in ctx["hol_scr_hw"].items():
            p.w(sync, nm, val)
        for oc in range(DC):
            par = oc % 2
            if oc >= 2:
                p.w(sync, "wg", wg0 + oc - 1)
            p.inc(sync.dma_start(out=t["WSB"][:, par * 2048: par * 2048 + 1024],
                                 in_=io["wscr_hol"][:, oc * 1024:(oc + 1) * 1024]), f"hb{par}")
            p.inc(sync.dma_start(out=t["WSB"][:, par * 2048 + 1024: par * 2048 + 2048],
                                 in_=io["wscr_hol"][:, 8192 + oc * 1024: 8192 + (oc + 1) * 1024]),
                  f"hb{par}")

    block.sync(b_ld)
    ev0 = p.n["ev"]

    PSV, PSW = t["PSV"], t["PSW"]

    def b_mm(tensor):
        p.w(tensor, "vE", ln_done)
        for g in range(DC * 2):
            oc, th = g // 2, g % 2
            par = oc % 2
            if th == 0:
                p.w(tensor, f"hb{par}", hb0[par] + 32 * (oc // 2 + 1))
            if g >= 2:
                p.w(tensor, "ev", ev0 + g - 1)
            rb = PSA if g % 2 == 0 else PSB
            ib = PSV if g % 2 == 0 else PSW
            for j in range(2):
                tt = th * 2 + j
                for kc in range(DC):
                    tensor.matmul(rb[:, j * 512:(j + 1) * 512],
                                  t["WSB"][:, par * 2048 + kc * 128: par * 2048 + kc * 128 + 128],
                                  LNB[:, kc * TC + tt * TTW: kc * TC + (tt + 1) * TTW],
                                  start=(kc == 0), stop=(kc == DC - 1))
                for kc in range(DC):
                    tensor.matmul(ib[:, j * 512:(j + 1) * 512],
                                  t["WSB"][:, par * 2048 + 1024 + kc * 128:
                                          par * 2048 + 1024 + kc * 128 + 128],
                                  LNB[:, kc * TC + tt * TTW: kc * TC + (tt + 1) * TTW],
                                  start=(kc == 0), stop=(kc == DC - 1))
            p.inc(tensor.drain(), "tE")
            ctx.setdefault(f"{tag}_mm_t", []).append(p.n["tE"])
            if th == 1:
                p.inc(tensor.drain(), "wg")

    block.tensor(b_mm)
    mm_ts = ctx[f"{tag}_mm_t"]

    def b_serial():
        for g in range(DC * 2):
            oc, th = g // 2, g % 2
            base = (g % 2) * 2048
            rb = PSA if g % 2 == 0 else PSB
            ib = PSV if g % 2 == 0 else PSW

            def s1(scalar, g=g, base=base, rb=rb, ib=ib):
                p.w(scalar, "tE", mm_ts[g])
                scalar.activation(t["WS"][:, base: base + 1024], rb[:, 0:1024], AF.Square)
                scalar.activation(t["WS"][:, base + 1024: base + 2048], ib[:, 0:1024],
                                  AF.Square)
                p.inc(scalar.drain(), "sE")

            block.scalar(s1)
            sv = p.n["sE"]

            def v1(vector, sv=sv, base=base):
                p.w(vector, "sE", sv)
                vector.tensor_add(t["WS"][:, base: base + 1024],
                                  t["WS"][:, base: base + 1024],
                                  t["WS"][:, base + 1024: base + 2048])
                p.inc(vector.drain(), "vE")

            block.vector(v1)
            vv = p.n["vE"]

            def s2(scalar, vv=vv, base=base):
                p.w(scalar, "vE", vv)
                scalar.activation(t["WS"][:, base + 1024: base + 2048],
                                  t["WS"][:, base: base + 1024], AF.Sqrt, scale=0.25)
                p.inc(scalar.drain(), "sE")

            block.scalar(s2)
            sv2 = p.n["sE"]

            def v2(vector, oc=oc, th=th, sv2=sv2, base=base):
                p.w(vector, "sE", sv2)
                cs = slice(oc * TC + th * 1024, oc * TC + (th + 1) * 1024)
                vector.tensor_add(X[:, cs], X[:, cs],
                                  t["WS"][:, base + 1024: base + 2048])
                p.inc(vector.drain(), "ev")

            block.vector(v2)

    b_serial()

    def b_fence(vector):
        p.w(vector, "ev", p.n["ev"])
        p.inc(vector.drain(), "vE")

    block.vector(b_fence)
    ctx[f"{tag}_done_v"] = p.n["vE"]


def ph_mixer(p, io, ctx):
    t, block = p.t, p.block
    X, LNB = t["X"], t["LNB"]
    PSA, PSB, PSV, PSW = t["PSA"], t["PSB"], t["PSV"], t["PSW"]
    ln_done = ctx["ln5_done_v"]

    def rslot(c):
        return t["VALB"][:, c * 2052: c * 2052 + 2048]

    def b_tanh(scalar):
        p.bar(scalar, ["dW"])
        scalar.activation(t["ACOL"][:], t["APC"][:], AF.Tanh)
        p.inc(scalar.drain(), "sE")

    block.scalar(b_tanh)

    s0, wg0, mx_cast, mx_cast_sE = stream_cast(p, io, ctx, "win_s", 8, "mxw",
                                               defer_casts=True)
    mx_cast(0)
    mx_cast(1)
    ev0 = p.n["ev"]

    def b_mm(tensor):
        p.w(tensor, "vE", ln_done)
        for outc in range(16):
            piece = outc // 2
            par = piece % 2
            wcol0 = par * 2048 + (outc % 2) * 1024
            for tt in range(NTT):
                u = outc * NTT + tt
                p.w(tensor, "wb", mx_cast_sE[piece])
                if u >= 8:
                    p.w(tensor, "ev", ev0 + u - 7)
                bank = (PSA, PSB, PSV, PSW)[(u % 8) // 2][:, (u % 2) * 512:(u % 2) * 512 + 512]
                for kc in range(DC):
                    tensor.matmul(bank, t["WSB"][:, wcol0 + kc * 128: wcol0 + kc * 128 + 128],
                                  LNB[:, kc * TC + tt * TTW: kc * TC + (tt + 1) * TTW],
                                  start=(kc == 0), stop=(kc == DC - 1))
                p.inc(tensor.drain(), "tE")
                ctx.setdefault("mx_mm_t", []).append(p.n["tE"])
            if outc % 2 == 1:
                p.inc(tensor.drain(), "wg")

    block.tensor(b_mm)
    mm_ts = ctx["mx_mm_t"]

    gs0 = [p.n["gs0"], p.n["gs1"]]
    ev_pe0 = p.n["ev"]

    def pe_one(outc):
        def b_pe(scalar):
            for tt in range(NTT):
                u = outc * NTT + tt
                p.w(scalar, "tE", mm_ts[u])
                bank = (PSA, PSB, PSV, PSW)[(u % 8) // 2][:, (u % 2) * 512:(u % 2) * 512 + 512]
                if outc < 8:
                    ng = u
                    par = ng % 2
                    if ng >= 2:
                        n2 = ng - 2
                        p.w(scalar, f"gs{n2 % 2}", gs0[n2 % 2] + 16 * (n2 // 2 + 1))
                    scalar.activation(t["GIN"][:, par * TTW:(par + 1) * TTW], bank, AF.Sigmoid)
                else:
                    c = outc - 8
                    scalar.copy(t["VALB"][:, c * 2052 + 2 + tt * TTW: c * 2052 + 2 + (tt + 1) * TTW],
                                bank)
                p.inc(scalar.drain(), "ev")

        block.scalar(b_pe)

    for outc in range(16):
        if outc % 2 == 0 and 2 <= outc // 2 + 1 < 8:
            mx_cast(outc // 2 + 1)
        pe_one(outc)

    def b_gst(gp):
        ng = 0
        for outc in range(8):
            for tt in range(NTT):
                u = outc * NTT + tt
                p.w(gp, "ev", ev_pe0 + u + 1)
                par = ng % 2
                p.inc(gp.dma_start(
                    out=io["gstage"][:, outc * TC + tt * TTW: outc * TC + (tt + 1) * TTW],
                    in_=t["GIN"][:, par * TTW:(par + 1) * TTW]), f"gs{par}")
                ng += 1

    block.gpsimd(b_gst)
    win_ev = p.n["ev"]

    def b_edge(scalar):
        p.w(scalar, "ev", win_ev)
        for c in range(DC):
            scalar.copy(t["EDGE"][:, c * 4: c * 4 + 2], t["VALB"][:, c * 2052 + 2: c * 2052 + 4])
            scalar.copy(t["EDGE"][:, c * 4 + 2: c * 4 + 4],
                        t["VALB"][:, c * 2052 + 2048: c * 2052 + 2050])
        p.inc(scalar.drain(), "sE")

    block.scalar(b_edge)
    eg_s = p.n["sE"]

    def b_eg_d(sync):
        p.w(sync, "sE", eg_s)
        p.inc(sync.dma_start(out=io["h3a_in"][:, :], in_=t["EDGE"][:]), "dW")

    block.sync(b_eg_d)
    eg_dW = p.n["dW"]

    def b_cc3a(gp):
        p.w(gp, "dW", eg_dW)
        cc_v = p.inc(gp.collective_compute(
            "AllGather", OP.bypass,
            replica_groups=[[0, 1], [2, 3], [4, 5], [6, 7]],
            ins=[io["h3a_in"].ap().opt()], outs=[io["h3a_out"].ap().opt()]), "cc")
        p.w(gp, "cc", cc_v)
        with gp.register("rn3a") as rn:
            gp.reg_load(rn, t["NBRI"][0:1, 0:1])
            nbr = gp.snap(rn, min_val=0, max_val=1)
            p.inc(gp.dma_start(out=t["R3A"][:],
                               in_=io["h3a_out"].ap()[bass.ds(nbr, 1)].squeeze(0)), "dG")

    block.gpsimd(b_cc3a)
    r3a_dG = p.n["dG"]

    def b_halo(vector):
        p.w(vector, "dG", r3a_dG)
        for c in range(DC):
            vector.tensor_scalar(t["VALB"][:, c * 2052: c * 2052 + 2],
                                 t["R3A"][:, c * 4 + 2: c * 4 + 4], t["LMC"][:], None, OP.mult)
            vector.tensor_scalar(t["VALB"][:, c * 2052 + 2050: c * 2052 + 2052],
                                 t["R3A"][:, c * 4: c * 4 + 2], t["RMC"][:], None, OP.mult)
        p.inc(vector.drain(), "vE")

    block.vector(b_halo)
    halo_v = p.n["vE"]

    gv0 = p.n["mk"]
    cl0 = [p.n["cl0"], p.n["cl1"]]
    cvs0 = p.n["sE"]

    gi0 = [p.n["gi0"], p.n["gi1"]]
    gs_tot = [p.n["gs0"], p.n["gs1"]]

    for c in range(DC):
        def b_cvl(sync, c=c):
            if c == 0:
                p.bar(sync, ["sE", "tE"])
            if c >= 2:
                p.w(sync, "sE", cvs0 + c - 1)
            p.inc(sync.dma_start(out=t["WS"][:, (c % 2) * 2048:(c % 2) * 2048 + 768],
                                 in_=io["convd_s"][:, c * 768:(c + 1) * 768]), f"cl{c % 2}")

        block.sync(b_cvl)

        def b_cvc(scalar, c=c):
            p.w(scalar, f"cl{c % 2}", cl0[c % 2] + 16 * (c // 2 + 1))
            if c >= 2:
                p.w(scalar, "mk", gv0 + (c - 2) * (NTT + 1) + NTT + 1)
            scalar.copy(t["CVDS"][:, (c % 2) * 768:((c % 2) + 1) * 768],
                        t["WS"][:, (c % 2) * 2048:(c % 2) * 2048 + 768])
            p.inc(scalar.drain(), "sE")

        block.scalar(b_cvc)

        def b_gld(sync, c=c):
            if c == 0:
                for i in range(2):
                    p.w(sync, f"gs{i}", gs_tot[i])
            for tt in range(NTT):
                u = c * NTT + tt
                if u >= 2:
                    p.w(sync, "mk", gv0 + (u - 2) // NTT * (NTT + 1) + ((u - 2) % NTT) + 1)
                p.inc(sync.dma_start(
                    out=t["GIN"][:, (u % 2) * TTW:((u % 2) + 1) * TTW],
                    in_=io["gstage"][:, c * TC + tt * TTW: c * TC + (tt + 1) * TTW]),
                    f"gi{u % 2}")

        block.sync(b_gld)

    def b_conv(tensor):
        p.w(tensor, "vE", halo_v)
        for c in range(DC):
            p.w(tensor, "sE", cvs0 + c + 1)
            for tt in range(NTT):
                u = c * NTT + tt
                if u >= 4:
                    p.w(tensor, "mk", gv0 + (u - 4) // NTT * (NTT + 1) + ((u - 4) % NTT) + 1)
                bank = (PSA, PSB)[(u % 4) // 2][:, (u % 2) * 512:(u % 2) * 512 + 512]
                for k in range(5):
                    tensor.matmul(bank,
                                  t["CVDS"][:, (c % 2) * 768 + k * 128:(c % 2) * 768 + k * 128 + 128],
                                  t["VALB"][:, c * 2052 + tt * TTW + k:
                                           c * 2052 + tt * TTW + k + 512],
                                  start=(k == 0), stop=False)
                tensor.matmul(bank, t["CVDS"][:, (c % 2) * 768 + 5 * 128:(c % 2) * 768 + 6 * 128],
                              t["ONEB"][:], start=False, stop=True)
                p.inc(tensor.drain(), "tE")
                ctx.setdefault("mx_cv_t", []).append(p.n["tE"])

    block.tensor(b_conv)
    cv_ts = ctx["mx_cv_t"]

    def b_gv(vector):
        k = 0
        for c in range(DC):
            cpar = c % 2
            for tt in range(NTT):
                p.w(vector, "tE", cv_ts[k]); k += 1
                u = c * NTT + tt
                p.w(vector, f"gi{u % 2}", gi0[u % 2] + 16 * (u // 2 + 1))
                bank = (PSA, PSB)[(u % 4) // 2][:, (u % 2) * 512:(u % 2) * 512 + 512]
                vector.tensor_mul(t["GVB"][:, cpar * 512 + 0: cpar * 512 + 512] if False
                                  else t["GVB"][:, tt * TTW:(tt + 1) * TTW],
                                  t["GIN"][:, (u % 2) * TTW:((u % 2) + 1) * TTW], bank)
                p.inc(vector.drain(), "mk")
            vector.drain()
            a_bc = t["ACOL"][:, c:c + 1].broadcast_to([128, TC])
            vector.tensor_tensor_scan(t["YS"][:, c * TC:(c + 1) * TC], a_bc,
                                      t["GVB"][:, 0:TC],
                                      0.0, OP.mult, OP.add)
            p.inc(vector.drain(), "mk")

    block.vector(b_gv)
    scan_mk = p.n["mk"]

    def b_carr(vector):
        p.w(vector, "mk", scan_mk)
        for c in range(DC):
            vector.tensor_copy(t["CARR"][:, c:c + 1], t["YS"][:, c * TC + TC - 1: c * TC + TC])
        p.inc(vector.drain(), "vE")

    block.vector(b_carr)
    carr_v = p.n["vE"]

    def b_carr_d(sync):
        p.w(sync, "vE", carr_v)
        p.inc(sync.dma_start(out=io["h3b_in"][:, :], in_=t["CARR"][:]), "dW")

    block.sync(b_carr_d)
    carr_dW = p.n["dW"]

    def b_cc3b(gp):
        p.w(gp, "dW", carr_dW)
        cc_v = p.inc(gp.collective_compute(
            "AllGather", OP.bypass,
            replica_groups=[[0, 1], [2, 3], [4, 5], [6, 7]],
            ins=[io["h3b_in"].ap().opt()], outs=[io["h3b_out"].ap().opt()]), "cc")
        p.w(gp, "cc", cc_v)
        with gp.register("rn3b") as rn:
            gp.reg_load(rn, t["NBRI"][0:1, 0:1])
            nbr = gp.snap(rn, min_val=0, max_val=1)
            p.inc(gp.dma_start(out=t["R3B"][:],
                               in_=io["h3b_out"].ap()[bass.ds(nbr, 1)].squeeze(0)), "dG")

    block.gpsimd(b_cc3b)
    r3b_dG = p.n["dG"]

    def b_corr(vector):
        p.w(vector, "dG", r3b_dG)
        for c in range(DC):
            vector.tensor_scalar(t["CARR"][:, c:c + 1], t["R3B"][:, c:c + 1],
                                 t["CMC"][:], None, OP.mult)
        p.inc(vector.drain(), "vE")
        for c in range(DC):
            a_bc = t["ACOL"][:, c:c + 1].broadcast_to([128, TC])
            z_bc = t["ZEROC"][:].broadcast_to([128, TC])
            vector.tensor_tensor_scan(t["POWT"][:], a_bc, z_bc, 1.0, OP.mult, OP.add)
            vector.drain()
            vector.scalar_tensor_tensor(t["YS"][:, c * TC:(c + 1) * TC], t["POWT"][:],
                                        t["CARR"][:, c:c + 1],
                                        t["YS"][:, c * TC:(c + 1) * TC], OP.mult, OP.add)
            vector.drain()
        p.inc(vector.drain(), "vE")

    block.vector(b_corr)
    corr_v = p.n["vE"]

    s2, wg2, _, cast_wb_mxo = stream_cast(p, io, ctx, "wout_s", 4, "mxo")
    ev2 = p.n["ev"]

    def b_wo(tensor):
        p.w(tensor, "vE", corr_v)
        for oc in range(DC):
            piece = oc // 2
            par = piece % 2
            wcol0 = par * 2048 + (oc % 2) * 1024
            for tt in range(NTT):
                u = oc * NTT + tt
                p.w(tensor, "wb", cast_wb_mxo[piece])
                if u >= 8:
                    p.w(tensor, "ev", ev2 + u - 7)
                bank = (PSA, PSB, PSV, PSW)[(u % 8) // 2][:, (u % 2) * 512:(u % 2) * 512 + 512]
                for kc in range(DC):
                    tensor.matmul(bank, t["WSB"][:, wcol0 + kc * 128: wcol0 + kc * 128 + 128],
                                  t["YS"][:, kc * TC + tt * TTW: kc * TC + (tt + 1) * TTW],
                                  start=(kc == 0), stop=(kc == DC - 1))
                p.inc(tensor.drain(), "tE")
                ctx.setdefault("mx_wo_t", []).append(p.n["tE"])
            if oc % 2 == 1:
                p.inc(tensor.drain(), "wg")

    block.tensor(b_wo)
    wo_ts = ctx["mx_wo_t"]

    def b_res(vector):
        k = 0
        for oc in range(DC):
            for tt in range(NTT):
                p.w(vector, "tE", wo_ts[k]); k += 1
                u = oc * NTT + tt
                bank = (PSA, PSB, PSV, PSW)[(u % 8) // 2][:, (u % 2) * 512:(u % 2) * 512 + 512]
                vector.tensor_add(rslot(oc)[:, tt * TTW:(tt + 1) * TTW],
                                  LNB[:, oc * TC + tt * TTW: oc * TC + (tt + 1) * TTW], bank)
                p.inc(vector.drain(), "ev")

    block.vector(b_res)

    def b_fence(vector):
        p.w(vector, "ev", p.n["ev"])
        p.inc(vector.drain(), "vE")

    block.vector(b_fence)

    ph_ln(p, io, ctx, "ln7", rsl=rslot)
    ctx["mx_done_v"] = ctx["ln7_done_v"]


def ph_store(p, io, ctx):
    t, block = p.t, p.block

    def b_st(sync):
        p.bar(sync, ["vE", "tE", "sE", "ev"])
        for c in range(DC):
            p.inc(sync.dma_start(out=io["y_out"][c * 128:(c + 1) * 128, :],
                                 in_=t["X"][:, c * TC:(c + 1) * TC]), "dW")
        p.w(sync, "dW", p.n["dW"])

    block.sync(b_st)



def _host_layouts(inputs, parity):
    out = {}
    Wqkv = inputs["L0_attn_Wqkv"]
    Wq, Wk, Wv = Wqkv[0:D], Wqkv[D:2 * D], Wqkv[2 * D:3 * D]

    def pair_chunks(W):
        cols = []
        for pair in range(NPAIR):
            Wp = W[pair * 128:(pair + 1) * 128]
            for kc in range(DC):
                cols.append(Wp[:, kc * 128:(kc + 1) * 128].T)
        return np.concatenate(cols, axis=1)

    def v_chunks(W):
        cols = []
        for grp in range(2):
            Wg = W[grp * 512:(grp + 1) * 512]
            for kc in range(DC):
                cols.append(Wg[:, kc * 128:(kc + 1) * 128].T)
        return np.concatenate(cols, axis=1)

    out["wqkv_s"] = np.ascontiguousarray(
        np.concatenate([pair_chunks(Wq), pair_chunks(Wk), v_chunks(Wv)], axis=1),
        dtype=np.float32)

    Wo = inputs["L0_attn_Wo"]
    cols = []
    for oc in range(DC):
        for af in range(DC):
            cols.append(Wo[oc * 128:(oc + 1) * 128, af * 128:(af + 1) * 128].T)
    out["wo_s"] = np.ascontiguousarray(np.concatenate(cols, axis=1), dtype=np.float32)

    Win = inputs["L1_attn_W_in"]
    cols = []
    for outc in range(16):
        for kc in range(DC):
            cols.append(Win[outc * 128:(outc + 1) * 128, kc * 128:(kc + 1) * 128].T)
    out["win_s"] = np.ascontiguousarray(np.concatenate(cols, axis=1), dtype=np.float32)

    Wout = inputs["L1_attn_W_out"]
    cols = []
    for oc in range(DC):
        for kc in range(DC):
            cols.append(Wout[oc * 128:(oc + 1) * 128, kc * 128:(kc + 1) * 128].T)
    out["wout_s"] = np.ascontiguousarray(np.concatenate(cols, axis=1), dtype=np.float32)

    gw = inputs["L0_ffn_gate_w"]
    cols = [gw[:, kc * 128:(kc + 1) * 128].T for kc in range(DC)]
    out["gw_s"] = np.ascontiguousarray(np.concatenate(cols, axis=1), dtype=np.float32)
    out["gb_s"] = np.ascontiguousarray(inputs["L0_ffn_gate_b"].reshape(1, 8), dtype=np.float32)
    out["eb_s"] = np.ascontiguousarray(inputs["L0_ffn_expert_b"], dtype=np.float32)

    ew = inputs["L0_ffn_expert_w"]
    moe = np.zeros((128, 65536), dtype=np.float32)
    col = 0
    for half in range(2):
        for e in range(E):
            blk = np.empty((128, 4096), dtype=np.float32)
            for oc in range(4):
                o0 = (half * 4 + oc) * 128
                for kc in range(DC):
                    blk[:, (oc * DC + kc) * 128:(oc * DC + kc) * 128 + 128] = \
                        ew[e, o0:o0 + 128, kc * 128:(kc + 1) * 128].T
            moe[:, col:col + 4096] = blk
            col += 4096
    out["moew_s"] = moe

    def hol_stream(W):
        cols = []
        for oc in range(DC):
            for kc in range(DC):
                cols.append(W[kc * 128:(kc + 1) * 128, oc * 128:(oc + 1) * 128])
        return np.ascontiguousarray(np.concatenate(cols, axis=1), dtype=np.float32)

    out["theta_s"] = hol_stream(inputs["L1_ffn_theta"])
    out["thn_s"] = hol_stream(inputs["L1_ffn_theta_noise"])
    out["phi_s"] = hol_stream(inputs["L1_ffn_phi"])
    out["phn_s"] = hol_stream(inputs["L1_ffn_phi_noise"])

    cw = inputs["L1_attn_conv_w"][:, 0, :]
    cb = inputs["L1_attn_conv_b"]
    cvd = np.zeros((128, 6144), dtype=np.float32)
    for c in range(DC):
        for k in range(6):
            blk = np.zeros((128, 128), dtype=np.float32)
            src = cw[c * 128:(c + 1) * 128, k] if k < 5 else cb[c * 128:(c + 1) * 128]
            np.fill_diagonal(blk, src)
            cvd[:, (c * 6 + k) * 128:(c * 6 + k) * 128 + 128] = blk
    out["convd_s"] = cvd

    out["ap_s"] = np.ascontiguousarray(inputs["L1_attn_a_param"].reshape(DC, 128).T,
                                       dtype=np.float32)
    out["lnw_s"] = np.ascontiguousarray(inputs["L1_attn_ln_w"].reshape(DC, 128).T,
                                        dtype=np.float32)
    out["lnb_s"] = np.ascontiguousarray(inputs["L1_attn_ln_b"].reshape(DC, 128).T,
                                        dtype=np.float32)

    frq = np.exp(-math.log(10000.0) * np.arange(0, DH, 2, dtype=np.float64) / DH)
    frq_col = np.zeros((128, 1), dtype=np.float32)
    sgn_col = np.zeros((128, 1), dtype=np.float32)
    for pp in range(2):
        for d in range(DH):
            frq_col[pp * 64 + d, 0] = frq[d // 2]
            sgn_col[pp * 64 + d, 0] = -1.0 if d % 2 == 0 else 1.0
    out["frq_s"] = frq_col
    out["sgn_s"] = sgn_col

    pos = np.arange(parity * TC, (parity + 1) * TC, dtype=np.float64)
    ang = frq_col.astype(np.float64) * pos[None, :]
    out["rsc_s"] = np.sin(ang).astype(np.float32)
    out["rcc_s"] = np.cos(ang).astype(np.float32)

    dm = np.zeros((128, 2048), dtype=np.float32)
    kk = np.arange(128)[:, None]
    qq = np.arange(512)[None, :]
    for off in range(4):
        dm[:, off * 512:(off + 1) * 512] = (qq >= off * 128 + kk).astype(np.float32)
    out["dmask_s"] = dm
    out["eye_s"] = np.eye(128, dtype=np.float32)
    out["rbias_s"] = np.full((128, 1), MASK_NEG if parity == 0 else 0.0, dtype=np.float32)
    out["lmask_s"] = np.full((128, 1), 1.0 if parity == 1 else 0.0, dtype=np.float32)
    out["rmask_s"] = np.full((128, 1), 1.0 if parity == 0 else 0.0, dtype=np.float32)
    out["cmask_s"] = np.full((128, 1), 1.0 if parity == 1 else 0.0, dtype=np.float32)
    out["nbr_s"] = np.array([[1 - parity]], dtype=np.int32)
    out["rtb_s"] = np.array([[parity * 8]], dtype=np.int32)
    return out


def kernel(**inputs):
    inputs = {k: np.asarray(v) for k, v in inputs.items()}
    nc = build_program()
    lay = [_host_layouts(inputs, parity) for parity in range(2)]
    x = inputs["L0_attn_x"]
    in_maps = []
    for core in range(8):
        b, half = core // 2, core % 2
        m = dict(lay[half])
        m["x_in"] = np.ascontiguousarray(x[b, half * TC:(half + 1) * TC, :].T,
                                         dtype=np.float32)
        m["pos_s"] = np.arange(half * TC, (half + 1) * TC,
                               dtype=np.float32).reshape(1, TC)
        in_maps.append(m)
    res = run_bass_kernel_spmd(nc, in_maps, core_ids=list(range(8)))
    out = np.empty((B, T, D), dtype=np.float32)
    for core in range(8):
        b, half = core // 2, core % 2
        out[b, half * TC:(half + 1) * TC, :] = res.results[core]["y_out"].T
    return out
